# version 15 (final)
# baseline (speedup 1.0000x reference)
import contextlib
import numpy as np
import ml_dtypes
import concourse.bass as bass
import concourse.mybir as mybir
from concourse.bass_utils import run_bass_kernel_spmd

F32 = mybir.dt.float32
BF16 = mybir.dt.bfloat16
AF = mybir.ActivationFunctionType
ALU = mybir.AluOpType
AX = mybir.AxisListType

NB = 8
S = 4096
D = 1024
NT = S // 128
ND = D // 128
NPAIR = 8
SQC = 512
NSQ = S // SQC
NSK = S // 128
HID = 4096
NHT = HID // 128
NHC = HID // 512
LN_EPS = 1e-5
QEPS = 1e-6
INV_D = 1.0 / D
INV_H = 1.0 / HID


class Sem:
    def __init__(self, h):
        self.h = h
        self.n = 0

    def inc(self, instr, k=1):
        instr.then_inc(self.h, k)
        self.n += k
        return self.n


def build_program(upto="D", debug=()):
    nc = bass.Bass("TRN2", target_bir_lowering=False)
    st = contextlib.ExitStack()

    din = lambda name, shape, dt=F32: nc.dram_tensor(name, shape, dt, kind="ExternalInput").ap()
    scr = lambda name, shape, dt=F32: nc.dram_tensor(name, shape, dt, kind="Internal").ap()

    xq = din("xq", [S, D])
    wq, wk, wv, wo = (din(n, [D, D]) for n in ("wq", "wk", "wv", "wo"))
    qb, kb, vb, ob = (din(n, [1, D]) for n in ("qb", "kb", "vb", "ob"))
    bwk, bwv, bwr, bwo = (din(n, [D, D]) for n in ("bwk", "bwv", "bwr", "bwo"))
    td, tf, tmk, tmv, tmr = (din(n, [1, D]) for n in ("td", "tf", "tmk", "tmv", "tmr"))
    wfc = din("wfc", [HID, D])
    lnw, lnb = din("lnw", [1, HID]), din("lnb", [1, HID])
    wproj = din("wproj", [D, HID])
    eye = din("eye", [128, 128], BF16)
    sel = din("sel", [33, 128])

    out = nc.dram_tensor("out", [S, D], F32, kind="ExternalOutput").ap()

    x1, x2, x3 = scr("x1", [S, D]), scr("x2", [S, D]), scr("x3", [S, D])
    wqbf, wkbf, wvbf, wobf = (scr(n, [D, D], BF16) for n in ("wqbf", "wkbf", "wvbf", "wobf"))
    bwkbf, bwvbf, bwrbf, bwobf = (scr(n, [D, D], BF16) for n in ("bwkbf", "bwvbf", "bwrbf", "bwobf"))
    bp_s = scr("bp_s", [1, D])
    fcT_s = scr("fcT_s", [128, ND * HID], BF16)
    pjT_s = scr("pjT_s", [128, NHT * D], BF16)

    dbg = {}
    dbg_specs = {
        "d_lnxT": ([128, ND * S], BF16),
        "d_ctxT": ([128, NPAIR * S], BF16),
        "d_qT": ([128, S], BF16),
        "d_kT": ([128, S], BF16),
        "d_rden": ([33, S], F32),
        "d_fcT": ([128, ND * HID], BF16),
        "d_wfcT": ([128, ND * HID], BF16),
        "d_bp": ([1, D], BF16),
        "d_pjT": ([128, NHT * D], BF16),
        "d_x1": ([S, D], F32),
        "d_x2": ([S, D], F32),
        "d_x3": ([S, D], F32),
    }
    for name in debug:
        shape, dt = dbg_specs[name]
        dbg[name] = nc.dram_tensor(name, shape, dt, kind="ExternalOutput").ap()

    sem_names = ("LDX", "LDW", "STO", "GP1", "MM1", "MM2", "MM3",
                 "EX1", "EX2", "EX3", "EV1", "EV2", "EV3",
                 "TRS", "LNX", "RSD", "NMR", "STA", "EVT", "HZV", "HZA")
    SM = {n: Sem(st.enter_context(nc.semaphore(n))) for n in sem_names}
    LDX, LDW, STO, GP1 = SM["LDX"], SM["LDW"], SM["STO"], SM["GP1"]
    MM1, MM2, MM3 = SM["MM1"], SM["MM2"], SM["MM3"]
    EX1, EX2, EX3 = SM["EX1"], SM["EX2"], SM["EX3"]
    EV1, EV2, EV3 = SM["EV1"], SM["EV2"], SM["EV3"]
    TRS, LNX, RSD, NMR, STA, EVT = (SM[n] for n in ("TRS", "LNX", "RSD", "NMR", "STA", "EVT"))
    HZV, HZA = SM["HZV"], SM["HZA"]

    def vg(eng, instr):
        HZV.inc(instr)
        eng.wait_ge(HZV.h, HZV.n)

    def ag(eng, instr):
        HZA.inc(instr)
        eng.wait_ge(HZA.h, HZA.n)

    def act_spacer(scalar):
        return scalar.activation(dumpb[:, 0:1], ones_col_f[:], AF.Identity)

    sb = lambda name, shape, dt: st.enter_context(nc.sbuf_tensor(name, shape, dt))

    eye_t = sb("eye_t", [128, 128], BF16)
    sel_t = sb("sel_t", [33, 128], F32)
    ones_r128f = sb("ones_r128f", [1, 128], F32)
    ones_r128b = sb("ones_r128b", [1, 128], BF16)
    ones_c1f = sb("ones_c1f", [1, 1], F32)
    ones_col_b = sb("ones_col_b", [128, 1], BF16)
    ones_col_f = sb("ones_col_f", [128, 1], F32)
    qbd8 = sb("qbd8", [128, ND], F32)
    kbc = sb("kbc", [128, ND], F32)
    vbrow = sb("vbrow", [1, D], BF16)
    vbc = sb("vbc", [128, ND], F32)
    obrow = sb("obrow", [1, D], BF16)
    lamc = sb("lamc", [128, ND], F32)
    euc = sb("euc", [128, ND], F32)
    tmkc = sb("tmkc", [128, ND], F32)
    tmvc = sb("tmvc", [128, ND], F32)
    tmrc = sb("tmrc", [128, ND], F32)
    omkc = sb("omkc", [128, ND], F32)
    omvc = sb("omvc", [128, ND], F32)
    omrc = sb("omrc", [128, ND], F32)
    tcol = sb("tcol", [128, ND], F32)
    tcol2 = sb("tcol2", [128, ND], F32)
    lnwc = sb("lnwc", [128, NHT], F32)
    lnbc = sb("lnbc", [128, NHT], BF16)
    thrq = sb("thrq", [128, 2], F32)
    nthrq = sb("nthrq", [128, 2], F32)
    bprow = sb("bprow", [1, D], BF16)

    s_sx = sb("s_sx", [128, 2], F32)
    s_sq = sb("s_sq", [128, 2], F32)
    s_mean = sb("s_mean", [128, 2], F32)
    s_var = sb("s_var", [128, 2], F32)
    s_r = sb("s_r", [128, 2], F32)
    s_nmr = sb("s_nmr", [128, 2], F32)
    s_tmp = sb("s_tmp", [128, 2], F32)
    dumpb = sb("dumpb", [128, 1536], BF16)

    def ln_sq_act(scalar, xap, par):
        return scalar.activation(dumpb[:, 0:xap.shape[-1]], xap, AF.Square,
                                 accum_out=s_sq[:, par:par + 1])

    def ln_var_dve(v, par, inv_n):
        p = slice(par, par + 1)
        vg(v, v.tensor_scalar_mul(s_mean[:, p], s_sx[:, p], inv_n))
        v.tensor_scalar_mul(s_var[:, p], s_sq[:, p], inv_n)
        vg(v, v.tensor_mul(s_tmp[:, p], s_mean[:, p], s_mean[:, p]))
        return v.scalar_tensor_tensor(s_var[:, p], s_var[:, p], LN_EPS, s_tmp[:, p],
                                      ALU.add, ALU.subtract)

    def ln_r_act(scalar, par):
        p = slice(par, par + 1)
        ag(scalar, scalar.activation(s_tmp[:, p], s_var[:, p], AF.Ln))
        return scalar.activation(s_r[:, p], s_tmp[:, p], AF.Exp, scale=-0.5)

    def ln_nmr_dve(v, par):
        p = slice(par, par + 1)
        return v.scalar_tensor_tensor(s_nmr[:, p], s_mean[:, p], -1.0, s_r[:, p],
                                      ALU.mult, ALU.mult)

    with contextlib.ExitStack() as ps:
        row_src = {"qb": qb, "kb": kb, "vb": vb, "ob": ob, "td": td, "tf": tf,
                   "tmk": tmk, "tmv": tmv, "tmr": tmr, "lnw": lnw, "lnb": lnb}

        with nc.Block() as blk:
            @blk.sync
            def _(sync):
                LDX.inc(sync.dma_start(eye_t[:], eye), 16)
                LDX.inc(sync.dma_start(sel_t[:], sel), 16)
                sync.wait_ge(LDX.h, LDX.n)

            @blk.gpsimd
            def _(g):
                g.memset(ones_r128f[:], 1.0)
                g.memset(ones_r128b[:], 1.0)
                g.memset(ones_c1f[:], 1.0)
                g.memset(ones_col_b[:], 1.0)
                g.memset(ones_col_f[:], 1.0)
                for src, dst in ((wq, wqbf), (wk, wkbf), (wv, wvbf), (wo, wobf),
                                 (bwk, bwkbf), (bwv, bwvbf), (bwr, bwrbf), (bwo, bwobf)):
                    GP1.inc(g.dma_start(dst, src), 16)
        gp1_casts = GP1.n
        gp1_qkv = gp1_casts - 5 * 16
        gp1_wo = gp1_casts - 4 * 16

        with contextlib.ExitStack() as qs:
            wtile = qs.enter_context(nc.sbuf_tensor("p_wtile", [128, 2, HID], F32))
            gtile = qs.enter_context(nc.sbuf_tensor("p_gtile", [128, HID], F32))
            ltile = qs.enter_context(nc.sbuf_tensor("p_ltile", [128, HID], F32))
            qtile = qs.enter_context(nc.sbuf_tensor("p_qtile", [128, 2, HID], BF16))
            wqT_p = qs.enter_context(nc.sbuf_tensor("p_wqT", [128, ND * HID], BF16))
            racc = qs.enter_context(nc.sbuf_tensor("p_racc", [128, 2], F32))
            rtmp = qs.enter_context(nc.sbuf_tensor("p_rtmp", [128, 1], F32))
            sthr = qs.enter_context(nc.sbuf_tensor("p_sthr", [1, 2], F32))

            gjobs = [(wfc, t, D, 0) for t in range(32)] + \
                    [(wproj, t, HID, 1) for t in range(8)]

            with nc.Block() as blk:
                dve_b, ldw_b = EV1.n, LDW.n

                @blk.sync
                def _(sync):
                    for cnt, (src, t, cols, acc_i) in enumerate(gjobs):
                        if cnt >= 2:
                            sync.wait_ge(EV1.h, dve_b + cnt - 1)
                        LDW.inc(sync.dma_start(wtile[:, cnt % 2, 0:cols],
                                               src[t * 128:(t + 1) * 128, :]), 16)

                @blk.vector
                def _(v):
                    v.memset(racc[:], 0.0)
                    for cnt, (src, t, cols, acc_i) in enumerate(gjobs):
                        v.wait_ge(LDW.h, ldw_b + 16 * (cnt + 1))
                        vg(v, v.tensor_reduce(rtmp[:], wtile[:, cnt % 2, 0:cols], axis=AX.X,
                                              op=ALU.add, apply_absolute_value=True))
                        EV1.inc(v.tensor_add(racc[:, acc_i:acc_i + 1],
                                             racc[:, acc_i:acc_i + 1], rtmp[:]))
                    v.wait_ge(EV1.h, dve_b + len(gjobs))

            with contextlib.ExitStack() as pps:
                p_tot = pps.enter_context(nc.psum_tensor("p_tot", [1, 2], F32))
                p_thr = pps.enter_context(nc.psum_tensor("p_thr", [128, 2], F32))
                with nc.Block() as blk:
                    @blk.tensor
                    def _(te):
                        MM1.inc(te.matmul(p_tot[:, 0:2], ones_col_f[:], racc[:, 0:2],
                                          start=True, stop=True))
                with nc.Block() as blk:
                    @blk.vector
                    def _(v):
                        v.wait_ge(MM1.h, MM1.n)
                        vg(v, v.tensor_scalar(sthr[:, 0:1], p_tot[:, 0:1], 0.5 / (HID * D),
                                              0.5 * QEPS, ALU.mult, ALU.add))
                        vg(v, v.tensor_scalar(sthr[:, 1:2], p_tot[:, 1:2], 0.5 / (HID * D),
                                              0.5 * QEPS, ALU.mult, ALU.add))
                with nc.Block() as blk:
                    @blk.tensor
                    def _(te):
                        MM1.inc(te.matmul(p_thr[:, 0:1], ones_r128f[:], sthr[:, 0:1],
                                          start=True, stop=True))
                        MM1.inc(te.matmul(p_thr[:, 1:2], ones_r128f[:], sthr[:, 1:2],
                                          start=True, stop=True))
                with nc.Block() as blk:
                    @blk.vector
                    def _(v):
                        v.wait_ge(MM1.h, MM1.n)
                        vg(v, v.tensor_copy(thrq[:], p_thr[:]))
                        vg(v, v.tensor_scalar_mul(nthrq[:], p_thr[:], -1.0))

            with contextlib.ExitStack() as pps:
                pqr = [pps.enter_context(nc.psum_tensor("p_pq%d" % i, [128, 64], F32))
                       for i in range(4)]

                for wsrc, dst_s, ntile, cols, acc_i in ((wfc, fcT_s, 32, D, 0),
                                                        (wproj, pjT_s, 8, HID, 1)):
                    nblk = cols // 128
                    stride = HID if cols == D else D
                    with nc.Block() as blk:
                        dve_b, ldw_b, mm3_b, ev2_b = EV1.n, LDW.n, MM3.n, EV2.n

                        @blk.sync
                        def _(sync):
                            for cnt in range(ntile):
                                if cnt >= 2:
                                    sync.wait_ge(EV1.h, dve_b + cnt - 1)
                                LDW.inc(sync.dma_start(wtile[:, cnt % 2, 0:cols],
                                                       wsrc[cnt * 128:(cnt + 1) * 128, :]), 16)

                        @blk.vector
                        def _(v):
                            for cnt in range(ntile):
                                v.wait_ge(LDW.h, ldw_b + 16 * (cnt + 1))
                                if cnt >= 2:
                                    v.wait_ge(MM3.h, mm3_b + nblk * (cnt - 1))
                                w = wtile[:, cnt % 2, 0:cols]
                                v.tensor_single_scalar(gtile[:, 0:cols], w,
                                                       thrq[:, acc_i:acc_i + 1], ALU.is_gt)
                                vg(v, v.tensor_single_scalar(ltile[:, 0:cols], w,
                                                             nthrq[:, acc_i:acc_i + 1],
                                                             ALU.is_lt))
                                EV1.inc(v.tensor_sub(qtile[:, cnt % 2, 0:cols],
                                                     gtile[:, 0:cols], ltile[:, 0:cols]))
                                for b in range(nblk):
                                    tgl = cnt * nblk + b
                                    v.wait_ge(MM3.h, mm3_b + tgl + 1)
                                    EV2.inc(v.tensor_copy(
                                        wqT_p[:, b * stride + cnt * 128:
                                              b * stride + (cnt + 1) * 128],
                                        pqr[tgl % 4][:].bitcast(BF16)))
                            v.wait_ge(EV2.h, ev2_b + ntile * nblk)

                        @blk.tensor
                        def _(te):
                            for cnt in range(ntile):
                                te.wait_ge(EV1.h, dve_b + cnt + 1)
                                for b in range(nblk):
                                    tgl = cnt * nblk + b
                                    if tgl >= 4:
                                        te.wait_ge(EV2.h, ev2_b + tgl - 3)
                                    MM3.inc(te.transpose(
                                        pqr[tgl % 4][:].bitcast(BF16),
                                        qtile[:, cnt % 2, b * 128:(b + 1) * 128],
                                        eye_t[:]))

                    with nc.Block() as blk:
                        @blk.gpsimd
                        def _(g):
                            GP1.inc(g.dma_start(dst_s, wqT_p[:, 0:32768]), 16)
                            g.wait_ge(GP1.h, GP1.n)

        rows = {}
        for name, width in (("qb", D), ("kb", D), ("vb", D), ("ob", D),
                            ("td", D), ("tf", D), ("tmk", D), ("tmv", D),
                            ("tmr", D), ("lnw", HID), ("lnb", HID)):
            rows[name] = ps.enter_context(nc.sbuf_tensor("r_" + name, [1, width], F32))
        with nc.Block() as blk:
            @blk.sync
            def _(sync):
                for name in row_src:
                    LDX.inc(sync.dma_start(rows[name][:], row_src[name]), 16)
                sync.wait_ge(LDX.h, LDX.n)

        with contextlib.ExitStack() as pps:
            p_c = pps.enter_context(nc.psum_tensor("p_c", [128, 64], F32))
            with nc.Block() as blk:
                @blk.tensor
                def _(te):
                    jobs = [("qb", 0), ("kb", 8), ("td", 16), ("tf", 24),
                            ("tmk", 32), ("tmv", 40), ("tmr", 48), ("vb", 56)]
                    for name, base in jobs:
                        for j in range(ND):
                            MM1.inc(te.matmul(p_c[:, base + j:base + j + 1],
                                              rows[name][0:1, j * 128:(j + 1) * 128],
                                              ones_c1f[:], start=True, stop=True))
            with nc.Block() as blk:
                @blk.vector
                def _(v):
                    v.wait_ge(MM1.h, MM1.n)
                    vg(v, v.tensor_scalar_mul(qbd8[:], p_c[:, 0:8], 0.125))
                    v.tensor_copy(kbc[:], p_c[:, 8:16])
                    v.tensor_copy(tcol[:], p_c[:, 16:24])
                    v.tensor_copy(tcol2[:], p_c[:, 24:32])
                    v.tensor_copy(tmkc[:], p_c[:, 32:40])
                    v.tensor_copy(tmvc[:], p_c[:, 40:48])
                    v.tensor_copy(tmrc[:], p_c[:, 48:56])
                    v.tensor_copy(vbc[:], p_c[:, 56:64])
                    for dst, srcc in ((omkc, tmkc), (omvc, tmvc), (omrc, tmrc)):
                        vg(v, v.tensor_scalar(dst[:], srcc[:], -1.0, 1.0, ALU.mult, ALU.add))
            with nc.Block() as blk:
                @blk.scalar
                def _(scalar):
                    ag(scalar, scalar.activation(lamc[:], tcol[:], AF.Exp))
            with nc.Block() as blk:
                @blk.vector
                def _(v):
                    vg(v, v.tensor_scalar_mul(tcol[:], lamc[:], -1.0))
            with nc.Block() as blk:
                @blk.scalar
                def _(scalar):
                    ag(scalar, scalar.activation(lamc[:], tcol[:], AF.Exp))
                    ag(scalar, scalar.activation(euc[:], tcol2[:], AF.Exp))
            with nc.Block() as blk:
                @blk.tensor
                def _(te):
                    for j in range(32):
                        te.matmul(p_c[:, j:j + 1], rows["lnw"][0:1, j * 128:(j + 1) * 128],
                                  ones_c1f[:], start=True, stop=True)
                    for j in range(32):
                        MM1.inc(te.matmul(p_c[:, 32 + j:33 + j],
                                          rows["lnb"][0:1, j * 128:(j + 1) * 128],
                                          ones_c1f[:], start=True, stop=True))
            with nc.Block() as blk:
                @blk.vector
                def _(v):
                    v.wait_ge(MM1.h, MM1.n)
                    vg(v, v.tensor_copy(lnwc[:], p_c[:, 0:32]))
                    vg(v, v.tensor_copy(lnbc[:], p_c[:, 32:64]))
                    vg(v, v.tensor_copy(vbrow[:], rows["vb"][:]))
                    vg(v, v.tensor_copy(obrow[:], rows["ob"][:]))

        if "d_fcT" in dbg or "d_pjT" in dbg:
            with nc.Block() as blk:
                @blk.gpsimd
                def _(g):
                    if "d_fcT" in dbg:
                        STO.inc(g.dma_start(dbg["d_fcT"], fcT_s), 16)
                    if "d_pjT" in dbg:
                        STO.inc(g.dma_start(dbg["d_pjT"], pjT_s), 16)

    if upto == "P":
        _finish(nc, STO)
        st.close()
        return nc

    def emit_ln_pass(xin_ap, lnxT_h, lnx_h, xa_h, pT_hs, n_tiles=NT, tile0=0,
                     col_off=0, keep_x=False, dve_pre=None, dve_post=None, hb=None):
        with nc.Block() as blk:
            ldx_b, ex3_b, sta_b, rsd_b, nmr_b, lnx_b, trs_b, evt_b, sto_b = (
                LDX.n, EX3.n, STA.n, RSD.n, NMR.n, LNX.n, TRS.n, EVT.n, STO.n)

            @blk.sync
            def _(sync):
                sync.wait_ge(STO.h, sto_b)
                for i in range(n_tiles):
                    if not keep_x and i >= 2:
                        sync.wait_ge(LNX.h, lnx_b + i - 1)
                    r0 = (tile0 + i) * 128
                    LDX.inc(sync.dma_start(xa_h[:, i if keep_x else i % 2, :],
                                           xin_ap[r0:r0 + 128, :]), 16)

            @blk.vector
            def _(v):
                if dve_pre is not None:
                    dve_pre(v)
                for i in range(n_tiles):
                    par = i % 2
                    xv = xa_h[:, i if keep_x else par, :]
                    v.wait_ge(LDX.h, ldx_b + 16 * (i + 1))
                    vg(v, v.reduce_sum(s_sx[:, par:par + 1], xv, axis=AX.X))
                    v.wait_ge(EX3.h, ex3_b + i + 1)
                    STA.inc(ln_var_dve(v, par, INV_D))
                    v.wait_ge(RSD.h, rsd_b + i + 1)
                    NMR.inc(ln_nmr_dve(v, par))
                    for j in range(ND):
                        t = ND * i + j
                        v.wait_ge(TRS.h, trs_b + t + 1)
                        EVT.inc(v.tensor_copy(
                            lnxT_h[:, j, col_off + i * 128:col_off + (i + 1) * 128],
                            pT_hs[t % 2][:]))
                if dve_post is not None:
                    dve_post(v)

            @blk.scalar
            def _(scalar):
                for i in range(n_tiles):
                    par = i % 2
                    xv = xa_h[:, i if keep_x else par, :]
                    scalar.wait_ge(LDX.h, ldx_b + 16 * (i + 1))
                    if i >= 2:
                        scalar.wait_ge(STA.h, sta_b + i - 1)
                    ln_sq_act(scalar, xv, par)
                    EX3.inc(act_spacer(scalar))
                    scalar.wait_ge(STA.h, sta_b + i + 1)
                    RSD.inc(ln_r_act(scalar, par))
                    scalar.wait_ge(NMR.h, nmr_b + i + 1)
                    LNX.inc(scalar.activation(lnx_h[:, par, :], xv, AF.Identity,
                                              scale=s_r[:, par:par + 1],
                                              bias=s_nmr[:, par:par + 1]))

            @blk.tensor
            def _(te):
                for i in range(n_tiles):
                    par = i % 2
                    if hb is not None:
                        te.wait_ge(STA.h, sta_b + i + 1)
                        for _ in range(14):
                            te.matmul(hb[:, 0:128], eye_t[:], eye_t[:],
                                      start=True, stop=True)
                        te.wait_ge(RSD.h, rsd_b + i + 1)
                        for _ in range(8):
                            te.matmul(hb[:, 0:128], eye_t[:], eye_t[:],
                                      start=True, stop=True)
                    te.wait_ge(LNX.h, lnx_b + i + 1)
                    for j in range(ND):
                        t = ND * i + j
                        if t >= 2:
                            te.wait_ge(EVT.h, evt_b + t - 1)
                        TRS.inc(te.transpose(pT_hs[t % 2][:],
                                             lnx_h[:, par, j * 128:(j + 1) * 128],
                                             eye_t[:]))

    with contextlib.ExitStack() as As:
        lnxT = As.enter_context(nc.sbuf_tensor("lnxT", [128, ND, S], BF16))
        ctxT = As.enter_context(nc.sbuf_tensor("ctxT", [128, NPAIR, S], BF16))
        rden = As.enter_context(nc.sbuf_tensor("rden", [33, S], F32))

        with contextlib.ExitStack() as A1:
            xa1 = A1.enter_context(nc.sbuf_tensor("a_xa", [128, 2, D], F32))
            lnx1 = A1.enter_context(nc.sbuf_tensor("a_lnx", [128, 2, D], BF16))
            pT0 = A1.enter_context(nc.psum_tensor("a_pT0", [128, 128], BF16))
            pT1 = A1.enter_context(nc.psum_tensor("a_pT1", [128, 128], BF16))
            a1hb = A1.enter_context(nc.psum_tensor("a_hb", [128, 512], F32))

            with nc.Block() as blk:
                @blk.vector
                def _(v):
                    v.memset(rden[:], 0.0)

            emit_ln_pass(xq, lnxT, lnx1, xa1, (pT0, pT1), hb=a1hb)

        if "d_lnxT" in dbg:
            with nc.Block() as blk:
                @blk.gpsimd
                def _(g):
                    STO.inc(g.dma_start(dbg["d_lnxT"],
                                        lnxT[:, :, :].rearrange("p a b -> p (a b)")), 16)

        if upto == "A1":
            _finish(nc, STO)
            As.close()
            st.close()
            return nc

        with contextlib.ExitStack() as A2:
            a2 = lambda name, shape, dt: A2.enter_context(nc.sbuf_tensor(name, shape, dt))
            wqT = a2("wqT", [128, ND, 128], BF16)
            wkT = a2("wkT", [128, ND, 128], BF16)
            wvT = a2("wvT", [128, ND, 128], BF16)
            qTt = a2("qTt", [128, S], BF16)
            kTt = a2("kTt", [128, S], BF16)
            sq2 = a2("sq2", [128, S], BF16)
            vTt = a2("vTt", [128, S], BF16)
            vsb = a2("vsb", [128, NSK, 130], BF16)
            Ebuf = a2("Ebuf", [128, 2, 1024], BF16)
            nmax8 = a2("nmax8", [1, 16], F32)
            nB = a2("nB", [1, 4], F32)
            negBcol = a2("negBcol", [128, 1], F32)
            PS0 = A2.enter_context(nc.psum_tensor("PS0", [128, 1024], F32))
            PS1 = A2.enter_context(nc.psum_tensor("PS1", [128, 1024], F32))
            ct0a = A2.enter_context(nc.psum_tensor("ct0a", [128, 512], F32))
            ct1a = A2.enter_context(nc.psum_tensor("ct1a", [128, 512], F32))
            ct0b = A2.enter_context(nc.psum_tensor("ct0b", [128, 512], F32))
            ct1b = A2.enter_context(nc.psum_tensor("ct1b", [128, 512], F32))
            PS = (PS0, PS1)
            CT0 = (ct0a, ct0b)
            CT1 = (ct1a, ct1b)

            with nc.Block() as blk:
                @blk.gpsimd
                def _(g):
                    g.memset(vsb[:, :, 64:65], 1.0)
                    g.memset(vsb[:, :, 129:130], 1.0)

            for p in range(NPAIR):
                with nc.Block() as blk:
                    ldw_b, mm1_b, mm2_b, mm3_b = LDW.n, MM1.n, MM2.n, MM3.n
                    ex1_b, ev1_b, ev2_b, ev3_b = EX1.n, EV1.n, EV2.n, EV3.n
                    rsd_b = RSD.n

                    @blk.sync
                    def _(sync):
                        if p == 0:
                            sync.wait_ge(GP1.h, gp1_casts)
                        for wbuf, wsrc in ((wqT, wqbf), (wkT, wkbf), (wvT, wvbf)):
                            for j in range(ND):
                                LDW.inc(sync.dma_start_transpose(
                                    wbuf[:, j, :], wsrc[p * 128:(p + 1) * 128,
                                                        j * 128:(j + 1) * 128]), 16)

                    @blk.tensor
                    def _(te):
                        for idx in range(24):
                            wt = (wqT, wkT, wvT)[idx // 8]
                            nn = idx % 8
                            te.wait_ge(LDW.h, ldw_b + 128 * (idx // 8 + 1))
                            if idx >= 2:
                                te.wait_ge(EX1.h, ex1_b + idx - 1)
                            for j in range(ND):
                                mm = te.matmul(PS[idx % 2][:, 0:512], wt[:, j, :],
                                               lnxT[:, j, nn * 512:(nn + 1) * 512],
                                               start=(j == 0), stop=(j == ND - 1))
                            MM1.inc(mm)
                        for m in range(NSK):
                            te.wait_ge(EX1.h, ex1_b + 16 + (m // 4) + 1)
                            if m >= 2:
                                te.wait_ge(EV1.h, ev1_b + m - 1)
                            MM2.inc(te.transpose(
                                CT0[m % 2][:, 0:64].bitcast(BF16),
                                vTt[:, m * 128:(m + 1) * 128], eye_t[:]))
                        for c in range(16):
                            te.wait_ge(EV2.h, ev2_b + (1 if c < 8 else 2))
                            if c >= 2:
                                te.wait_ge(EV3.h, ev3_b + c - 1)
                            else:
                                te.wait_ge(EX1.h, ex1_b + 24)
                            MM3.inc(te.matmul(PS[c % 2][0:1, 0:512], ones_col_b[:],
                                              sq2[:, (c % 8) * 512:(c % 8 + 1) * 512],
                                              start=True, stop=True))
                        te.wait_ge(RSD.h, rsd_b + 1)
                        MM3.inc(te.matmul(PS[0][0:128, 512:513], ones_r128f[:],
                                          nB[0:1, 3:4], start=True, stop=True))

                    @blk.scalar
                    def _(scalar):
                        for idx in range(24):
                            scalar.wait_ge(MM1.h, mm1_b + idx + 1)
                            nn = idx % 8
                            if idx < 8:
                                EX1.inc(scalar.activation(
                                    qTt[:, nn * 512:(nn + 1) * 512], PS[idx % 2][:, 0:512],
                                    AF.Identity, scale=0.125, bias=qbd8[:, p:p + 1]))
                            elif idx < 16:
                                EX1.inc(scalar.activation(
                                    kTt[:, nn * 512:(nn + 1) * 512], PS[idx % 2][:, 0:512],
                                    AF.Identity, bias=kbc[:, p:p + 1]))
                            else:
                                EX1.inc(scalar.activation(
                                    vTt[:, nn * 512:(nn + 1) * 512], PS[idx % 2][:, 0:512],
                                    AF.Identity, bias=vbc[:, p:p + 1]))
                        scalar.wait_ge(EV3.h, ev3_b + 17)
                        ag(scalar, scalar.activation(nB[0:1, 2:3], nB[0:1, 1:2], AF.Ln))
                        RSD.inc(scalar.activation(nB[0:1, 3:4], nB[0:1, 2:3],
                                                  AF.Exp, scale=0.5))

                    @blk.vector
                    def _(v):
                        for m in range(NSK):
                            v.wait_ge(MM2.h, mm2_b + m + 1)
                            pv = CT0[m % 2][:, 0:64].bitcast(BF16)
                            v.tensor_copy(vsb[:, m, 0:64], pv[:, 0:64])
                            EV1.inc(v.tensor_copy(vsb[:, m, 65:129], pv[:, 64:128]))
                        v.wait_ge(EX1.h, ex1_b + 8)
                        EV2.inc(v.tensor_mul(sq2[:], qTt[:], qTt[:]))
                        for c in range(8):
                            v.wait_ge(MM3.h, mm3_b + c + 1)
                            EV3.inc(v.reduce_max(nmax8[0:1, c:c + 1], PS[c % 2][0:1, 0:512],
                                                 axis=AX.X))
                        v.wait_ge(EX1.h, ex1_b + 16)
                        EV2.inc(v.tensor_mul(sq2[:], kTt[:], kTt[:]))
                        for c in range(8, 16):
                            v.wait_ge(MM3.h, mm3_b + c + 1)
                            EV3.inc(v.reduce_max(nmax8[0:1, c:c + 1], PS[c % 2][0:1, 0:512],
                                                 axis=AX.X))
                        vg(v, v.tensor_copy(s_tmp[:, 0:1], s_tmp[:, 0:1]))
                        v.reduce_max(nB[0:1, 0:1], nmax8[0:1, 0:8], axis=AX.X)
                        vg(v, v.reduce_max(nB[0:1, 1:2], nmax8[0:1, 8:16], axis=AX.X))
                        EV3.inc(v.tensor_mul(nB[0:1, 1:2], nB[0:1, 1:2], nB[0:1, 0:1]))
                        v.wait_ge(MM3.h, mm3_b + 17)
                        v.tensor_scalar_mul(negBcol[:], PS[0][0:128, 512:513], -1.0)

                with nc.Block() as blk:
                    mm1_b, mm2_b, ex1_b, ev1_b = MM1.n, MM2.n, EX1.n, EV1.n

                    def emit_ctx(te, j):
                        n2, sk2 = divmod(j, 32)
                        par2 = n2 % 2
                        te.wait_ge(EX1.h, ex1_b + j + 1)
                        if sk2 == 0 and n2 >= 2:
                            te.wait_ge(EV1.h, ev1_b + n2 - 1)
                        st_ = (sk2 == 0)
                        sp = (sk2 == NSK - 1)
                        te.matmul(CT0[par2][0:65, :], vsb[:, sk2, 0:65],
                                  Ebuf[:, j % 2, 0:512], start=st_, stop=sp,
                                  tile_position=(0, 0))
                        MM2.inc(te.matmul(CT1[par2][0:65, :], vsb[:, sk2, 65:130],
                                          Ebuf[:, j % 2, 512:1024], start=st_, stop=sp,
                                          tile_position=(0, 0)))

                    @blk.tensor
                    def _(te):
                        for idx in range(256):
                            n, sk = divmod(idx, 32)
                            if idx >= 2:
                                te.wait_ge(EX1.h, ex1_b + idx - 1)
                            te.matmul(PS[idx % 2][:, 0:512],
                                      kTt[0:64, sk * 128:(sk + 1) * 128],
                                      qTt[0:64, n * 512:(n + 1) * 512],
                                      start=True, stop=True, tile_position=(0, 0))
                            MM1.inc(te.matmul(PS[idx % 2][:, 512:1024],
                                              kTt[64:128, sk * 128:(sk + 1) * 128],
                                              qTt[64:128, n * 512:(n + 1) * 512],
                                              start=True, stop=True, tile_position=(64, 0)))
                            if idx >= 1:
                                emit_ctx(te, idx - 1)
                        emit_ctx(te, 255)

                    @blk.scalar
                    def _(scalar):
                        for idx in range(256):
                            scalar.wait_ge(MM1.h, mm1_b + idx + 1)
                            if idx >= 2:
                                scalar.wait_ge(MM2.h, mm2_b + idx - 1)
                            EX1.inc(scalar.activation(Ebuf[:, idx % 2, :],
                                                      PS[idx % 2][:, 0:1024],
                                                      AF.Exp, bias=negBcol[:]))

                    @blk.vector
                    def _(v):
                        for n in range(NSQ):
                            par2 = n % 2
                            v.wait_ge(MM2.h, mm2_b + 32 * (n + 1))
                            v.reciprocal(rden[0:1, n * 512:(n + 1) * 512],
                                         CT0[par2][64:65, :])
                            v.reciprocal(rden[32:33, n * 512:(n + 1) * 512],
                                         CT1[par2][64:65, :])
                            v.tensor_copy(ctxT[0:64, p, n * 512:(n + 1) * 512],
                                          CT0[par2][0:64, :])
                            EV1.inc(v.tensor_copy(ctxT[64:128, p, n * 512:(n + 1) * 512],
                                                  CT1[par2][0:64, :]))

                with nc.Block() as blk:
                    mm3_b, ev2_b = MM3.n, EV2.n

                    @blk.tensor
                    def _(te):
                        for m in range(NT):
                            if m >= 2:
                                te.wait_ge(EV2.h, ev2_b + m - 1)
                            MM3.inc(te.matmul(PS[m % 2][0:128, 0:128], sel_t[:],
                                              rden[0:33, m * 128:(m + 1) * 128],
                                              start=True, stop=True))

                    @blk.vector
                    def _(v):
                        for m in range(NT):
                            v.wait_ge(MM3.h, mm3_b + m + 1)
                            EV2.inc(v.tensor_tensor(ctxT[:, p, m * 128:(m + 1) * 128],
                                                    PS[m % 2][0:128, 0:128],
                                                    ctxT[:, p, m * 128:(m + 1) * 128],
                                                    ALU.mult))

            if "d_qT" in dbg:
                with nc.Block() as blk:
                    @blk.gpsimd
                    def _(g):
                        STO.inc(g.dma_start(dbg["d_qT"], qTt[:]), 16)
                        STO.inc(g.dma_start(dbg["d_kT"], kTt[:]), 16)

        if "d_ctxT" in dbg:
            with nc.Block() as blk:
                @blk.gpsimd
                def _(g):
                    STO.inc(g.dma_start(dbg["d_ctxT"],
                                        ctxT[:, :, :].rearrange("p a b -> p (a b)")), 16)
        if "d_rden" in dbg:
            with nc.Block() as blk:
                @blk.gpsimd
                def _(g):
                    STO.inc(g.dma_start(dbg["d_rden"], rden[:]), 16)

        with contextlib.ExitStack() as A5:
            woT = A5.enter_context(nc.sbuf_tensor("woT", [128, ND, D], BF16))
            xa5 = A5.enter_context(nc.sbuf_tensor("xa5", [128, 2, D], F32))
            xo5 = A5.enter_context(nc.sbuf_tensor("xo5", [128, 2, D], F32))
            po0 = A5.enter_context(nc.psum_tensor("po0", [128, 512], F32))
            po1 = A5.enter_context(nc.psum_tensor("po1", [128, 512], F32))
            PO = (po0, po1)

            with nc.Block() as blk:
                ldw_b, ldx_b, mm1_b, ev1_b, sto_b = LDW.n, LDX.n, MM1.n, EV1.n, STO.n

                @blk.sync
                def _(sync):
                    for j in range(ND):
                        LDW.inc(sync.dma_start_transpose(
                            woT[:, j, :], wobf[:, j * 128:(j + 1) * 128]), 16)
                    for m in range(NT):
                        if m >= 2:
                            sync.wait_ge(EV1.h, ev1_b + 2 * (m - 1))
                        LDX.inc(sync.dma_start(xa5[:, m % 2, :],
                                               xq[m * 128:(m + 1) * 128, :]), 16)

                @blk.tensor
                def _(te):
                    te.wait_ge(LDW.h, ldw_b + 128)
                    for m in range(NT):
                        for oc in range(2):
                            t = 2 * m + oc
                            if t >= 2:
                                te.wait_ge(EV1.h, ev1_b + t - 1)
                            for pp in range(NPAIR):
                                te.matmul(PO[oc][:, :], ctxT[:, pp, m * 128:(m + 1) * 128],
                                          woT[:, pp, oc * 512:(oc + 1) * 512],
                                          start=(pp == 0), stop=False)
                            MM1.inc(te.matmul(PO[oc][:, :], ones_r128b[:],
                                              obrow[0:1, oc * 512:(oc + 1) * 512],
                                              start=False, stop=True))

                @blk.vector
                def _(v):
                    for m in range(NT):
                        v.wait_ge(LDX.h, ldx_b + 16 * (m + 1))
                        if m >= 2:
                            v.wait_ge(STO.h, sto_b + 16 * (m - 1))
                        for oc in range(2):
                            t = 2 * m + oc
                            v.wait_ge(MM1.h, mm1_b + t + 1)
                            EV1.inc(v.tensor_add(xo5[:, m % 2, oc * 512:(oc + 1) * 512],
                                                 PO[oc][:, :],
                                                 xa5[:, m % 2, oc * 512:(oc + 1) * 512]))

                @blk.gpsimd
                def _(g):
                    for m in range(NT):
                        g.wait_ge(EV1.h, ev1_b + 2 * (m + 1))
                        STO.inc(g.dma_start(x1[m * 128:(m + 1) * 128, :],
                                            xo5[:, m % 2, :]), 16)
                    g.wait_ge(STO.h, STO.n)

    if "d_x1" in dbg:
        with nc.Block() as blk:
            @blk.gpsimd
            def _(g):
                g.wait_ge(STO.h, STO.n)
                STO.inc(g.dma_start(dbg["d_x1"], x1), 16)

    if upto == "A":
        _finish(nc, STO)
        st.close()
        return nc

    def emit_ffn(tag, xin_ap, xout_ap, dbg_w=None, dbg_bp=None):
        with contextlib.ExitStack() as Bs:
            bsb = lambda name, shape, dt: Bs.enter_context(
                nc.sbuf_tensor(tag + name, shape, dt))
            WfcT = bsb("WfcT", [128, ND, HID], BF16)
            WpjT = bsb("WpjT", [128, NHT, D], BF16)
            bpc = bsb("bpc", [128, ND], F32)
            xa = bsb("xa", [128, 3, D], F32)
            xo = bsb("xo", [128, 2, D], F32)
            lnx = bsb("lnx", [128, 2, D], BF16)
            lxT = bsb("lxT", [128, 2, ND, 128], BF16)
            hbuf = bsb("h", [128, HID], BF16)
            hnb = bsb("hn", [128, HID], BF16)
            hnT = bsb("hnT", [128, 2, NHT, 128], BF16)
            sumh8 = bsb("sumh8", [128, 2, NHC], F32)
            sumq8 = bsb("sumq8", [128, 2, NHC], F32)
            pT0 = Bs.enter_context(nc.psum_tensor(tag + "pT0", [128, 128], BF16))
            pT1 = Bs.enter_context(nc.psum_tensor(tag + "pT1", [128, 128], BF16))
            pF0 = Bs.enter_context(nc.psum_tensor(tag + "pF0", [128, 512], F32))
            pF1 = Bs.enter_context(nc.psum_tensor(tag + "pF1", [128, 512], F32))
            pF2 = Bs.enter_context(nc.psum_tensor(tag + "pF2", [128, 512], F32))
            pF3 = Bs.enter_context(nc.psum_tensor(tag + "pF3", [128, 512], F32))
            pP0 = Bs.enter_context(nc.psum_tensor(tag + "pP0", [128, 512], F32))
            pP1 = Bs.enter_context(nc.psum_tensor(tag + "pP1", [128, 512], F32))
            pB = pP0[:, 0:ND]
            pTs = (pT0, pT1)
            pFs = (pF0, pF1, pF2, pF3)
            pPs = (pP0, pP1)

            with nc.Block() as blk:
                ldw_b = LDW.n

                @blk.sync
                def _(sync):
                    sync.wait_ge(STO.h, STO.n)
                    wfc_flat = WfcT[:, :, :].rearrange("p a b -> p (a b)")
                    wpj_flat = WpjT[:, :, :].rearrange("p a b -> p (a b)")
                    for c8 in range(8):
                        LDW.inc(sync.dma_start(wfc_flat[:, c8 * 4096:(c8 + 1) * 4096],
                                               fcT_s[:, c8 * 4096:(c8 + 1) * 4096]), 16)
                    for c8 in range(8):
                        LDW.inc(sync.dma_start(wpj_flat[:, c8 * 4096:(c8 + 1) * 4096],
                                               pjT_s[:, c8 * 4096:(c8 + 1) * 4096]), 16)
                    sync.wait_ge(LDW.h, LDW.n)

                @blk.tensor
                def _(te):
                    for c16 in range(16):
                        te.wait_ge(LDW.h, ldw_b + 16 * c16)
                        for _ in range(12):
                            te.matmul(pF3[:, 0:128], eye_t[:], eye_t[:],
                                      start=True, stop=True)
            with nc.Block() as blk:
                @blk.tensor
                def _(te):
                    for dt_ in range(ND):
                        for t in range(NHT):
                            mm = te.matmul(pB[:, dt_:dt_ + 1],
                                           WpjT[:, t, dt_ * 128:(dt_ + 1) * 128],
                                           lnbc[:, t:t + 1],
                                           start=(t == 0), stop=(t == NHT - 1))
                            if t == NHT - 1:
                                MM1.inc(mm)
            with nc.Block() as blk:
                @blk.vector
                def _(v):
                    v.wait_ge(MM1.h, MM1.n)
                    vg(v, v.tensor_copy(bpc[:], pB))
            with nc.Block() as blk:
                @blk.gpsimd
                def _(g):
                    GP1.inc(g.dma_start(
                        bp_s.rearrange("o (j p) -> p (o j)", p=128), bpc[:],
                        allow_slow_non_contiguous=True), 16)
                    g.wait_ge(GP1.h, GP1.n)
                    GP1.inc(g.dma_start(bprow[:], bp_s), 16)
                    g.wait_ge(GP1.h, GP1.n)

            if dbg_w is not None:
                with nc.Block() as blk:
                    @blk.gpsimd
                    def _(g):
                        STO.inc(g.dma_start(dbg_w, WfcT[:, :, :].rearrange("p a b -> p (a b)")), 16)
                        STO.inc(g.dma_start(dbg_bp, bprow[:]), 16)
                        g.wait_ge(STO.h, STO.n)
            with nc.Block() as blk:
                (ldx_b, ex3_b, sta_b, rsd_b, nmr_b, lnx_b, trs_b, evt_b, mm1_b,
                 mm2_b, ex1_b, ex2_b, ev1_b, sto_b) = (
                    LDX.n, EX3.n, STA.n, RSD.n, NMR.n, LNX.n, TRS.n, EVT.n,
                    MM1.n, MM2.n, EX1.n, EX2.n, EV1.n, STO.n)
                stax = lambda k: 2 * k + 1
                stah = lambda i: 2 * i + 2
                lx_idx, hn_idx = {}, {}
                _seq = 0
                for _k in range(NT + 1):
                    if _k < NT:
                        for _j in range(ND):
                            lx_idx[(_k, _j)] = _seq
                            _seq += 1
                    if _k >= 1:
                        for _tt in range(NHT):
                            hn_idx[(_k - 1, _tt)] = _seq
                            _seq += 1

                @blk.sync
                def _(sync):
                    for i in range(NT):
                        if i >= 3:
                            sync.wait_ge(EV1.h, ev1_b + 2 * (i - 2))
                        LDX.inc(sync.dma_start(xa[:, i % 3, :],
                                               xin_ap[i * 128:(i + 1) * 128, :]), 16)

                @blk.gpsimd
                def _(g):
                    for i in range(NT):
                        g.wait_ge(EV1.h, ev1_b + 2 * (i + 1))
                        STO.inc(g.dma_start(xout_ap[i * 128:(i + 1) * 128, :],
                                            xo[:, i % 2, :]), 16)
                    g.wait_ge(STO.h, STO.n)

                @blk.vector
                def _(v):
                    for k in range(NT + 1):
                        par = k % 2
                        if k < NT:
                            v.wait_ge(LDX.h, ldx_b + 16 * (k + 1))
                            vg(v, v.reduce_sum(s_sx[:, par:par + 1], xa[:, k % 3, :],
                                               axis=AX.X))
                            v.wait_ge(EX3.h, ex3_b + k + 1)
                            STA.inc(ln_var_dve(v, par, INV_D))
                            v.wait_ge(RSD.h, rsd_b + stax(k))
                            NMR.inc(ln_nmr_dve(v, par))
                            for j in range(ND):
                                t = lx_idx[(k, j)]
                                v.wait_ge(TRS.h, trs_b + t + 1)
                                EVT.inc(v.tensor_copy(lxT[:, par, j, :], pTs[t % 2][:]))
                        if k >= 1:
                            i = k - 1
                            pi = i % 2
                            for tt in range(NHT):
                                t = hn_idx[(i, tt)]
                                v.wait_ge(TRS.h, trs_b + t + 1)
                                EVT.inc(v.tensor_scalar_mul(hnT[:, pi, tt, :],
                                                            pTs[t % 2][:],
                                                            lnwc[:, tt:tt + 1]))
                        if k < NT:
                            for hc in range(NHC):
                                v.wait_ge(EX1.h, ex1_b + 16 * k + hc + 1)
                                hs = v.scalar_tensor_tensor(
                                    dumpb[:, 1024:1536], hbuf[:, hc * 512:(hc + 1) * 512], 1.0,
                                    hbuf[:, hc * 512:(hc + 1) * 512], ALU.mult, ALU.mult,
                                    accum_out=sumq8[:, par, hc:hc + 1])
                            vg(v, hs)
                            v.wait_ge(EX2.h, ex2_b + k + 1)
                            v.reduce_sum(s_sx[:, par:par + 1], sumh8[:, par, :], axis=AX.X)
                            vg(v, v.reduce_sum(s_sq[:, par:par + 1], sumq8[:, par, :],
                                               axis=AX.X))
                            STA.inc(ln_var_dve(v, par, INV_H))
                            v.wait_ge(RSD.h, rsd_b + stah(k))
                            NMR.inc(ln_nmr_dve(v, par))
                        if k >= 1:
                            i = k - 1
                            pi = i % 2
                            if i >= 2:
                                v.wait_ge(STO.h, sto_b + 16 * (i - 1))
                            for oc in range(2):
                                v.wait_ge(MM2.h, mm2_b + 2 * i + oc + 1)
                                EV1.inc(v.tensor_add(xo[:, pi, oc * 512:(oc + 1) * 512],
                                                     pPs[oc][:, :],
                                                     xa[:, i % 3, oc * 512:(oc + 1) * 512]))

                @blk.scalar
                def _(scalar):
                    for k in range(NT + 1):
                        par = k % 2
                        if k < NT:
                            scalar.wait_ge(LDX.h, ldx_b + 16 * (k + 1))
                            if k >= 2:
                                scalar.wait_ge(STA.h, sta_b + stah(k - 2))
                            ln_sq_act(scalar, xa[:, k % 3, :], par)
                            EX3.inc(act_spacer(scalar))
                            scalar.wait_ge(STA.h, sta_b + stax(k))
                            RSD.inc(ln_r_act(scalar, par))
                            scalar.wait_ge(NMR.h, nmr_b + stax(k))
                            if k >= 2:
                                scalar.wait_ge(TRS.h, trs_b + lx_idx[(k - 2, ND - 1)] + 1)
                            LNX.inc(scalar.activation(lnx[:, par, :], xa[:, k % 3, :],
                                                      AF.Identity,
                                                      scale=s_r[:, par:par + 1],
                                                      bias=s_nmr[:, par:par + 1]))
                        if k >= 1:
                            i = k - 1
                            pi = i % 2
                            scalar.wait_ge(NMR.h, nmr_b + stah(i))
                            if i >= 1:
                                scalar.wait_ge(TRS.h, trs_b + hn_idx[(i - 1, NHT - 1)] + 1)
                            for hc in range(NHC):
                                EX1.inc(scalar.activation(hnb[:, hc * 512:(hc + 1) * 512],
                                                          hbuf[:, hc * 512:(hc + 1) * 512],
                                                          AF.Identity,
                                                          scale=s_r[:, pi:pi + 1],
                                                          bias=s_nmr[:, pi:pi + 1]))
                        if k < NT:
                            for hc in range(NHC):
                                scalar.wait_ge(MM1.h, mm1_b + 8 * k + hc + 1)
                                EX1.inc(scalar.activation(
                                    hbuf[:, hc * 512:(hc + 1) * 512],
                                    pFs[hc % 4][:, :], AF.Gelu,
                                    accum_out=sumh8[:, par, hc:hc + 1]))
                            EX2.inc(act_spacer(scalar))
                            scalar.wait_ge(STA.h, sta_b + stah(k))
                            RSD.inc(ln_r_act(scalar, par))

                @blk.tensor
                def _(te):
                    for k in range(NT + 1):
                        par = k % 2
                        if k < NT:
                            te.wait_ge(LNX.h, lnx_b + k + 1)
                            for j in range(ND):
                                t = lx_idx[(k, j)]
                                if t >= 2:
                                    te.wait_ge(EVT.h, evt_b + t - 1)
                                TRS.inc(te.transpose(pTs[t % 2][:],
                                                     lnx[:, par, j * 128:(j + 1) * 128],
                                                     eye_t[:]))
                            te.wait_ge(EVT.h, evt_b + lx_idx[(k, ND - 1)] + 1)
                            for hc in range(NHC):
                                g_ = 8 * k + hc
                                if g_ >= 4:
                                    ip, hp = divmod(g_ - 4, 8)
                                    te.wait_ge(EX1.h, ex1_b + 16 * ip + hp + 1)
                                for j in range(ND):
                                    mm = te.matmul(pFs[hc % 4][:, :],
                                                   lxT[:, par, j, :],
                                                   WfcT[:, j, hc * 512:(hc + 1) * 512],
                                                   start=(j == 0), stop=(j == ND - 1))
                                MM1.inc(mm)
                        if k >= 1:
                            i = k - 1
                            pi = i % 2
                            for tt in range(NHT):
                                t = hn_idx[(i, tt)]
                                te.wait_ge(EX1.h, ex1_b + 16 * i + 8 + (tt // 4) + 1)
                                if t >= 2:
                                    te.wait_ge(EVT.h, evt_b + t - 1)
                                TRS.inc(te.transpose(pTs[t % 2][:],
                                                     hnb[:, tt * 128:(tt + 1) * 128],
                                                     eye_t[:]))
                            te.wait_ge(EVT.h, evt_b + hn_idx[(i, NHT - 1)] + 1)
                            for oc in range(2):
                                t2 = 2 * i + oc
                                if t2 >= 2:
                                    te.wait_ge(EV1.h, ev1_b + t2 - 1)
                                for tt in range(NHT):
                                    te.matmul(pPs[oc][:, :], hnT[:, pi, tt, :],
                                              WpjT[:, tt, oc * 512:(oc + 1) * 512],
                                              start=(tt == 0), stop=False)
                                MM2.inc(te.matmul(pPs[oc][:, :], ones_r128b[:],
                                                  bprow[0:1, oc * 512:(oc + 1) * 512],
                                                  start=False, stop=True))

    emit_ffn("b_", x1, x2, dbg.get("d_wfcT"), dbg.get("d_bp"))
    if "d_x2" in dbg:
        with nc.Block() as blk:
            @blk.gpsimd
            def _(g):
                g.wait_ge(STO.h, STO.n)
                STO.inc(g.dma_start(dbg["d_x2"], x2), 16)
    if upto == "B":
        _finish(nc, STO)
        st.close()
        return nc

    with contextlib.ExitStack() as Cs:
        csb = lambda name, shape, dt: Cs.enter_context(nc.sbuf_tensor(name, shape, dt))
        WkT4 = csb("WkT4", [128, ND, D], BF16)
        WvT4 = csb("WvT4", [128, ND, D], BF16)
        WrT4 = csb("WrT4", [128, ND, D], BF16)
        WoT4 = csb("WoT4", [128, ND, D], BF16)
        lamt = csb("lamt", [128, ND, 512], F32)
        ones512 = csb("ones512", [128, 512], F32)
        lnxTc = csb("lnxTc", [128, ND, 513], BF16)
        xkT = csb("xkT", [128, ND, 512], BF16)
        xvT = csb("xvT", [128, ND, 512], BF16)
        xrT = csb("xrT", [128, ND, 512], BF16)
        Abuf = csb("Abuf", [128, ND, 513], F32)
        Bbuf = csb("Bbuf", [128, ND, 513], F32)
        Ec = csb("Ec", [128, 512], F32)
        Vc = csb("Vc", [128, 512], F32)
        Xc = csb("Xc", [128, 512], F32)
        SRc = csb("SRc", [128, 512], F32)
        TN = csb("TN", [128, 512], F32)
        TD = csb("TD", [128, 512], F32)
        rwT = csb("rwT", [128, ND, 512], BF16)
        xac = csb("c_xa", [128, 4, D], F32)
        xoc = csb("c_xo", [128, 2, D], F32)
        lnxc = csb("c_lnx", [128, 2, D], BF16)
        cpT0 = Cs.enter_context(nc.psum_tensor("c_pT0", [128, 128], BF16))
        cpT1 = Cs.enter_context(nc.psum_tensor("c_pT1", [128, 128], BF16))
        cpK = Cs.enter_context(nc.psum_tensor("c_pK", [128, 512], F32))
        cpV = Cs.enter_context(nc.psum_tensor("c_pV", [128, 512], F32))
        cpR = Cs.enter_context(nc.psum_tensor("c_pR", [128, 512], F32))
        cpO0 = Cs.enter_context(nc.psum_tensor("c_pO0", [128, 512], F32))
        cpO1 = Cs.enter_context(nc.psum_tensor("c_pO1", [128, 512], F32))
        chb = Cs.enter_context(nc.psum_tensor("c_hb", [128, 512], F32))
        cpTs = (cpT0, cpT1)
        cpOs = (cpO0, cpO1)

        with nc.Block() as blk:
            sto_b0 = STO.n

            @blk.sync
            def _(sync):
                sync.wait_ge(STO.h, sto_b0)
                for wsrc, wdst in ((bwkbf, WkT4), (bwvbf, WvT4), (bwrbf, WrT4), (bwobf, WoT4)):
                    for j in range(ND):
                        LDW.inc(sync.dma_start_transpose(
                            wdst[:, j, :], wsrc[:, j * 128:(j + 1) * 128]), 16)
                sync.wait_ge(LDW.h, LDW.n)

            @blk.gpsimd
            def _(g):
                g.wait_ge(STO.h, sto_b0)
                g.memset(ones512[:], 1.0)
                g.memset(lnxTc[:, :, 0:1], 0.0)
                g.memset(Abuf[:, :, 512:513], 0.0)
                g.memset(Bbuf[:, :, 512:513], 0.0)

        with nc.Block() as blk:
            @blk.vector
            def _(v):
                for j in range(ND):
                    v.tensor_scalar_mul(lamt[:, j, :], ones512[:], lamc[:, j:j + 1])

        for c in range(NSQ):
            def c_pre(v, c=c):
                if c > 0:
                    v.tensor_copy(lnxTc[:, :, 0:1], lnxTc[:, :, 512:513])
                v.tensor_copy(Abuf[:, :, 0:1], Abuf[:, :, 512:513])
                v.tensor_copy(Bbuf[:, :, 0:1], Bbuf[:, :, 512:513])

            def c_post(v, c=c):
                for j in range(ND):
                    cur = lnxTc[:, j, 1:513]
                    sh = lnxTc[:, j, 0:512]
                    for dst, tc_, oc_ in ((xkT, tmkc, omkc), (xvT, tmvc, omvc),
                                          (xrT, tmrc, omrc)):
                        v.tensor_scalar_mul(TN[:], sh, oc_[:, j:j + 1])
                        v.scalar_tensor_tensor(dst[:, j, :], cur, tc_[:, j:j + 1],
                                               TN[:], ALU.mult, ALU.add)

            emit_ln_pass(x2, lnxTc, lnxc, xac, cpTs, n_tiles=4, tile0=4 * c,
                         col_off=1, keep_x=True, dve_pre=c_pre, dve_post=c_post,
                         hb=chb)

            with nc.Block() as blk:
                mm1_b, ex1_b, ex2_b, ev1_b, ev2_b, ev3_b = (
                    MM1.n, EX1.n, EX2.n, EV1.n, EV2.n, EV3.n)

                @blk.tensor
                def _(te):
                    for j in range(ND):
                        if j >= 1:
                            te.wait_ge(EX1.h, ex1_b + j)
                            te.wait_ge(EV2.h, ev2_b + j)
                            te.wait_ge(EX2.h, ex2_b + j)
                        for jin in range(ND):
                            mmk = te.matmul(cpK[:, :], WkT4[:, jin, j * 128:(j + 1) * 128],
                                            xkT[:, jin, :], start=(jin == 0), stop=(jin == ND - 1))
                        MM1.inc(mmk)
                        for jin in range(ND):
                            mmv = te.matmul(cpV[:, :], WvT4[:, jin, j * 128:(j + 1) * 128],
                                            xvT[:, jin, :], start=(jin == 0), stop=(jin == ND - 1))
                        MM1.inc(mmv)
                        for jin in range(ND):
                            mmr = te.matmul(cpR[:, :], WrT4[:, jin, j * 128:(j + 1) * 128],
                                            xrT[:, jin, :], start=(jin == 0), stop=(jin == ND - 1))
                        MM1.inc(mmr)

                @blk.scalar
                def _(scalar):
                    for j in range(ND):
                        scalar.wait_ge(MM1.h, mm1_b + 3 * j + 1)
                        if j >= 1:
                            scalar.wait_ge(EV3.h, ev3_b + j)
                        EX1.inc(scalar.activation(Ec[:], cpK[:, :], AF.Exp))
                        scalar.wait_ge(MM1.h, mm1_b + 3 * j + 3)
                        if j >= 1:
                            scalar.wait_ge(EV1.h, ev1_b + j)
                        EX2.inc(scalar.activation(SRc[:], cpR[:, :], AF.Tanh, scale=0.5))

                @blk.vector
                def _(v):
                    for j in range(ND):
                        v.wait_ge(MM1.h, mm1_b + 3 * j + 2)
                        EV2.inc(v.tensor_copy(Vc[:], cpV[:, :]))
                        v.wait_ge(EX1.h, ex1_b + j + 1)
                        v.tensor_mul(Xc[:], Ec[:], Vc[:])
                        v.tensor_tensor_scan(Abuf[:, j, 1:513], lamt[:, j, :], Xc[:],
                                             Abuf[:, j, 0:1], ALU.mult, ALU.add)
                        v.tensor_tensor_scan(Bbuf[:, j, 1:513], lamt[:, j, :], Ec[:],
                                             Bbuf[:, j, 0:1], ALU.mult, ALU.add)
                        v.scalar_tensor_tensor(TN[:], Xc[:], euc[:, j:j + 1],
                                               Abuf[:, j, 0:512], ALU.mult, ALU.add)
                        EV3.inc(v.scalar_tensor_tensor(TD[:], Ec[:], euc[:, j:j + 1],
                                                       Bbuf[:, j, 0:512], ALU.mult, ALU.add))
                        v.reciprocal(TD[:], TD[:])
                        v.tensor_mul(TN[:], TN[:], TD[:])
                        v.wait_ge(EX2.h, ex2_b + j + 1)
                        v.tensor_scalar(SRc[:], SRc[:], 0.5, 0.5, ALU.mult, ALU.add)
                        EV1.inc(v.tensor_mul(rwT[:, j, :], TN[:], SRc[:]))

            with nc.Block() as blk:
                mm2_b, ev1_b, sto_b = MM2.n, EV1.n, STO.n

                @blk.tensor
                def _(te):
                    for m in range(4):
                        for oc in range(2):
                            t = 2 * m + oc
                            if t >= 2:
                                te.wait_ge(EV1.h, ev1_b + t - 1)
                            for j in range(ND):
                                mm = te.matmul(cpOs[oc][:, :], rwT[:, j, m * 128:(m + 1) * 128],
                                               WoT4[:, j, oc * 512:(oc + 1) * 512],
                                               start=(j == 0), stop=(j == ND - 1))
                            MM2.inc(mm)

                @blk.vector
                def _(v):
                    for m in range(4):
                        v.wait_ge(STO.h, sto_b + 16 * (m - 1) if m >= 2 else sto_b)
                        for oc in range(2):
                            t = 2 * m + oc
                            v.wait_ge(MM2.h, mm2_b + t + 1)
                            EV1.inc(v.tensor_add(xoc[:, m % 2, oc * 512:(oc + 1) * 512],
                                                 cpOs[oc][:, :],
                                                 xac[:, m, oc * 512:(oc + 1) * 512]))

                @blk.gpsimd
                def _(g):
                    for m in range(4):
                        g.wait_ge(EV1.h, ev1_b + 2 * (m + 1))
                        STO.inc(g.dma_start(
                            x3[(4 * c + m) * 128:(4 * c + m + 1) * 128, :],
                            xoc[:, m % 2, :]), 16)
                    if c == NSQ - 1:
                        g.wait_ge(STO.h, STO.n)

    if "d_x3" in dbg:
        with nc.Block() as blk:
            @blk.gpsimd
            def _(g):
                g.wait_ge(STO.h, STO.n)
                STO.inc(g.dma_start(dbg["d_x3"], x3), 16)
    if upto == "C":
        _finish(nc, STO)
        st.close()
        return nc

    emit_ffn("d_", x3, out)

    _finish(nc, STO)
    st.close()
    return nc


def _finish(nc, STO):
    with nc.Block() as blk:
        @blk.sync
        def _(sync):
            sync.wait_ge(STO.h, STO.n)


_SEL = None
_EYE = None


def _consts():
    global _SEL, _EYE
    if _SEL is None:
        s = np.zeros((33, 128), dtype=np.float32)
        s[0, 0:64] = 1.0
        s[32, 64:128] = 1.0
        _SEL = s
        _EYE = np.eye(128, dtype=ml_dtypes.bfloat16)
    return _SEL, _EYE


def make_in_maps(inputs):
    f32 = np.float32
    q = np.ascontiguousarray(np.asarray(inputs["L0_attn_a_query"], dtype=f32))
    selv, eyev = _consts()
    shared = {
        "wq": np.ascontiguousarray(np.asarray(inputs["L0_attn_a_q_w"], f32)),
        "wk": np.ascontiguousarray(np.asarray(inputs["L0_attn_a_k_w"], f32)),
        "wv": np.ascontiguousarray(np.asarray(inputs["L0_attn_a_v_w"], f32)),
        "wo": np.ascontiguousarray(np.asarray(inputs["L0_attn_a_out_w"], f32)),
        "qb": np.asarray(inputs["L0_attn_a_q_b"], f32).reshape(1, -1),
        "kb": np.asarray(inputs["L0_attn_a_k_b"], f32).reshape(1, -1),
        "vb": np.asarray(inputs["L0_attn_a_v_b"], f32).reshape(1, -1),
        "ob": np.asarray(inputs["L0_attn_a_out_b"], f32).reshape(1, -1),
        "bwk": np.ascontiguousarray(np.asarray(inputs["L0_attn_b_Wk"], f32)),
        "bwv": np.ascontiguousarray(np.asarray(inputs["L0_attn_b_Wv"], f32)),
        "bwr": np.ascontiguousarray(np.asarray(inputs["L0_attn_b_Wr"], f32)),
        "bwo": np.ascontiguousarray(np.asarray(inputs["L0_attn_b_Wo"], f32)),
        "td": np.asarray(inputs["L0_attn_b_time_decay"], f32).reshape(1, -1),
        "tf": np.asarray(inputs["L0_attn_b_time_first"], f32).reshape(1, -1),
        "tmk": np.asarray(inputs["L0_attn_b_time_mix_k"], f32).reshape(1, -1),
        "tmv": np.asarray(inputs["L0_attn_b_time_mix_v"], f32).reshape(1, -1),
        "tmr": np.asarray(inputs["L0_attn_b_time_mix_r"], f32).reshape(1, -1),
        "wfc": np.ascontiguousarray(np.asarray(inputs["L0_ffn_W_fc"], f32)),
        "lnw": np.asarray(inputs["L0_ffn_ln_w"], f32).reshape(1, -1),
        "lnb": np.asarray(inputs["L0_ffn_ln_b"], f32).reshape(1, -1),
        "wproj": np.ascontiguousarray(np.asarray(inputs["L0_ffn_W_proj"], f32)),
        "eye": eyev,
        "sel": selv,
    }
    maps = []
    for b in range(NB):
        m = dict(shared)
        m["xq"] = np.ascontiguousarray(q[b])
        maps.append(m)
    return maps


def kernel(**inputs):
    nc = build_program(upto="D")
    in_maps = make_in_maps(inputs)
    res = run_bass_kernel_spmd(nc, in_maps, core_ids=list(range(NB)))
    return np.stack([np.asarray(r["out"], dtype=np.float32) for r in res.results], axis=0)


# version 16
# speedup vs baseline: 1.0155x; 1.0155x over previous
import contextlib
import numpy as np
import ml_dtypes
import concourse.bass as bass
import concourse.mybir as mybir
from concourse.bass_utils import run_bass_kernel_spmd

F32 = mybir.dt.float32
BF16 = mybir.dt.bfloat16
AF = mybir.ActivationFunctionType
ALU = mybir.AluOpType
AX = mybir.AxisListType

NB = 8
S = 4096
D = 1024
NT = S // 128
ND = D // 128
NPAIR = 8
SQC = 512
NSQ = S // SQC
NSK = S // 128
HID = 4096
NHT = HID // 128
NHC = HID // 512
LN_EPS = 1e-5
QEPS = 1e-6
INV_D = 1.0 / D
INV_H = 1.0 / HID


class Sem:
    def __init__(self, h):
        self.h = h
        self.n = 0

    def inc(self, instr, k=1):
        instr.then_inc(self.h, k)
        self.n += k
        return self.n


def build_program(upto="D", debug=()):
    nc = bass.Bass("TRN2", target_bir_lowering=False)
    st = contextlib.ExitStack()

    din = lambda name, shape, dt=F32: nc.dram_tensor(name, shape, dt, kind="ExternalInput").ap()
    scr = lambda name, shape, dt=F32: nc.dram_tensor(name, shape, dt, kind="Internal").ap()

    xq = din("xq", [S, D])
    wq, wk, wv, wo = (din(n, [D, D]) for n in ("wq", "wk", "wv", "wo"))
    qb, kb, vb, ob = (din(n, [1, D]) for n in ("qb", "kb", "vb", "ob"))
    bwk, bwv, bwr, bwo = (din(n, [D, D]) for n in ("bwk", "bwv", "bwr", "bwo"))
    td, tf, tmk, tmv, tmr = (din(n, [1, D]) for n in ("td", "tf", "tmk", "tmv", "tmr"))
    wfc = din("wfc", [HID, D])
    lnw, lnb = din("lnw", [1, HID]), din("lnb", [1, HID])
    wproj = din("wproj", [D, HID])
    eye = din("eye", [128, 128], BF16)
    sel = din("sel", [33, 128])

    out = nc.dram_tensor("out", [S, D], F32, kind="ExternalOutput").ap()

    x1, x2, x3 = scr("x1", [S, D]), scr("x2", [S, D]), scr("x3", [S, D])
    wqbf, wkbf, wvbf, wobf = (scr(n, [D, D], BF16) for n in ("wqbf", "wkbf", "wvbf", "wobf"))
    bwkbf, bwvbf, bwrbf, bwobf = (scr(n, [D, D], BF16) for n in ("bwkbf", "bwvbf", "bwrbf", "bwobf"))
    bp_s = scr("bp_s", [1, D])
    fcT_s = scr("fcT_s", [128, ND * HID], BF16)
    pjT_s = scr("pjT_s", [128, NHT * D], BF16)

    dbg = {}
    dbg_specs = {
        "d_lnxT": ([128, ND * S], BF16),
        "d_ctxT": ([128, NPAIR * S], BF16),
        "d_qT": ([128, S], BF16),
        "d_kT": ([128, S], BF16),
        "d_rden": ([33, S], F32),
        "d_fcT": ([128, ND * HID], BF16),
        "d_wfcT": ([128, ND * HID], BF16),
        "d_bp": ([1, D], BF16),
        "d_pjT": ([128, NHT * D], BF16),
        "d_x1": ([S, D], F32),
        "d_x2": ([S, D], F32),
        "d_x3": ([S, D], F32),
    }
    for name in debug:
        shape, dt = dbg_specs[name]
        dbg[name] = nc.dram_tensor(name, shape, dt, kind="ExternalOutput").ap()

    sem_names = ("LDX", "LDW", "STO", "GP1", "MM1", "MM2", "MM3",
                 "EX1", "EX2", "EX3", "EV1", "EV2", "EV3",
                 "TRS", "LNX", "RSD", "NMR", "STA", "EVT", "HZV", "HZA")
    SM = {n: Sem(st.enter_context(nc.semaphore(n))) for n in sem_names}
    LDX, LDW, STO, GP1 = SM["LDX"], SM["LDW"], SM["STO"], SM["GP1"]
    MM1, MM2, MM3 = SM["MM1"], SM["MM2"], SM["MM3"]
    EX1, EX2, EX3 = SM["EX1"], SM["EX2"], SM["EX3"]
    EV1, EV2, EV3 = SM["EV1"], SM["EV2"], SM["EV3"]
    TRS, LNX, RSD, NMR, STA, EVT = (SM[n] for n in ("TRS", "LNX", "RSD", "NMR", "STA", "EVT"))
    HZV, HZA = SM["HZV"], SM["HZA"]

    def vg(eng, instr):
        HZV.inc(instr)
        eng.wait_ge(HZV.h, HZV.n)

    def ag(eng, instr):
        HZA.inc(instr)
        eng.wait_ge(HZA.h, HZA.n)

    def act_spacer(scalar):
        return scalar.activation(dumpb[:, 0:1], ones_col_f[:], AF.Identity)

    sb = lambda name, shape, dt: st.enter_context(nc.sbuf_tensor(name, shape, dt))

    eye_t = sb("eye_t", [128, 128], BF16)
    sel_t = sb("sel_t", [33, 128], F32)
    ones_r128f = sb("ones_r128f", [1, 128], F32)
    ones_r128b = sb("ones_r128b", [1, 128], BF16)
    ones_c1f = sb("ones_c1f", [1, 1], F32)
    ones_col_b = sb("ones_col_b", [128, 1], BF16)
    ones_col_f = sb("ones_col_f", [128, 1], F32)
    qbd8 = sb("qbd8", [128, ND], F32)
    kbc = sb("kbc", [128, ND], F32)
    vbrow = sb("vbrow", [1, D], BF16)
    vbc = sb("vbc", [128, ND], F32)
    obrow = sb("obrow", [1, D], BF16)
    lamc = sb("lamc", [128, ND], F32)
    euc = sb("euc", [128, ND], F32)
    tmkc = sb("tmkc", [128, ND], F32)
    tmvc = sb("tmvc", [128, ND], F32)
    tmrc = sb("tmrc", [128, ND], F32)
    omkc = sb("omkc", [128, ND], F32)
    omvc = sb("omvc", [128, ND], F32)
    omrc = sb("omrc", [128, ND], F32)
    tcol = sb("tcol", [128, ND], F32)
    tcol2 = sb("tcol2", [128, ND], F32)
    lnwc = sb("lnwc", [128, NHT], F32)
    lnbc = sb("lnbc", [128, NHT], BF16)
    thrq = sb("thrq", [128, 2], F32)
    nthrq = sb("nthrq", [128, 2], F32)
    bprow = sb("bprow", [1, D], BF16)

    s_sx = sb("s_sx", [128, 2], F32)
    s_sq = sb("s_sq", [128, 2], F32)
    s_mean = sb("s_mean", [128, 2], F32)
    s_var = sb("s_var", [128, 2], F32)
    s_r = sb("s_r", [128, 2], F32)
    s_nmr = sb("s_nmr", [128, 2], F32)
    s_tmp = sb("s_tmp", [128, 2], F32)
    dumpb = sb("dumpb", [128, 1536], BF16)

    def ln_sq_act(scalar, xap, par):
        return scalar.activation(dumpb[:, 0:xap.shape[-1]], xap, AF.Square,
                                 accum_out=s_sq[:, par:par + 1])

    def ln_var_dve(v, par, inv_n):
        p = slice(par, par + 1)
        vg(v, v.tensor_scalar_mul(s_mean[:, p], s_sx[:, p], inv_n))
        v.tensor_scalar_mul(s_var[:, p], s_sq[:, p], inv_n)
        vg(v, v.tensor_mul(s_tmp[:, p], s_mean[:, p], s_mean[:, p]))
        return v.scalar_tensor_tensor(s_var[:, p], s_var[:, p], LN_EPS, s_tmp[:, p],
                                      ALU.add, ALU.subtract)

    def ln_r_act(scalar, par):
        p = slice(par, par + 1)
        ag(scalar, scalar.activation(s_tmp[:, p], s_var[:, p], AF.Ln))
        return scalar.activation(s_r[:, p], s_tmp[:, p], AF.Exp, scale=-0.5)

    def ln_nmr_dve(v, par):
        p = slice(par, par + 1)
        return v.scalar_tensor_tensor(s_nmr[:, p], s_mean[:, p], -1.0, s_r[:, p],
                                      ALU.mult, ALU.mult)

    with contextlib.ExitStack() as ps:
        row_src = {"qb": qb, "kb": kb, "vb": vb, "ob": ob, "td": td, "tf": tf,
                   "tmk": tmk, "tmv": tmv, "tmr": tmr, "lnw": lnw, "lnb": lnb}

        with nc.Block() as blk:
            @blk.sync
            def _(sync):
                LDX.inc(sync.dma_start(eye_t[:], eye), 16)
                LDX.inc(sync.dma_start(sel_t[:], sel), 16)
                sync.wait_ge(LDX.h, LDX.n)

            @blk.gpsimd
            def _(g):
                g.memset(ones_r128f[:], 1.0)
                g.memset(ones_r128b[:], 1.0)
                g.memset(ones_c1f[:], 1.0)
                g.memset(ones_col_b[:], 1.0)
                g.memset(ones_col_f[:], 1.0)
                for src, dst in ((wq, wqbf), (wk, wkbf), (wv, wvbf), (wo, wobf),
                                 (bwk, bwkbf), (bwv, bwvbf), (bwr, bwrbf), (bwo, bwobf)):
                    GP1.inc(g.dma_start(dst, src), 16)
        gp1_casts = GP1.n
        gp1_qkv = gp1_casts - 5 * 16
        gp1_wo = gp1_casts - 4 * 16

        with contextlib.ExitStack() as qs:
            wtile = qs.enter_context(nc.sbuf_tensor("p_wtile", [128, 2, HID], F32))
            gtile = qs.enter_context(nc.sbuf_tensor("p_gtile", [128, HID], F32))
            ltile = qs.enter_context(nc.sbuf_tensor("p_ltile", [128, HID], F32))
            qtile = qs.enter_context(nc.sbuf_tensor("p_qtile", [128, 2, HID], BF16))
            wqT_p = qs.enter_context(nc.sbuf_tensor("p_wqT", [128, ND * HID], BF16))
            racc = qs.enter_context(nc.sbuf_tensor("p_racc", [128, 2], F32))
            rtmp = qs.enter_context(nc.sbuf_tensor("p_rtmp", [128, 1], F32))
            sthr = qs.enter_context(nc.sbuf_tensor("p_sthr", [1, 2], F32))

            gjobs = [(wfc, t, D, 0) for t in range(32)] + \
                    [(wproj, t, HID, 1) for t in range(8)]

            with nc.Block() as blk:
                dve_b, ldw_b = EV1.n, LDW.n

                @blk.sync
                def _(sync):
                    for cnt, (src, t, cols, acc_i) in enumerate(gjobs):
                        if cnt >= 2:
                            sync.wait_ge(EV1.h, dve_b + cnt - 1)
                        LDW.inc(sync.dma_start(wtile[:, cnt % 2, 0:cols],
                                               src[t * 128:(t + 1) * 128, :]), 16)

                @blk.vector
                def _(v):
                    v.memset(racc[:], 0.0)
                    for cnt, (src, t, cols, acc_i) in enumerate(gjobs):
                        v.wait_ge(LDW.h, ldw_b + 16 * (cnt + 1))
                        vg(v, v.tensor_reduce(rtmp[:], wtile[:, cnt % 2, 0:cols], axis=AX.X,
                                              op=ALU.add, apply_absolute_value=True))
                        EV1.inc(v.tensor_add(racc[:, acc_i:acc_i + 1],
                                             racc[:, acc_i:acc_i + 1], rtmp[:]))
                    v.wait_ge(EV1.h, dve_b + len(gjobs))

            with contextlib.ExitStack() as pps:
                p_tot = pps.enter_context(nc.psum_tensor("p_tot", [1, 2], F32))
                p_thr = pps.enter_context(nc.psum_tensor("p_thr", [128, 2], F32))
                with nc.Block() as blk:
                    @blk.tensor
                    def _(te):
                        MM1.inc(te.matmul(p_tot[:, 0:2], ones_col_f[:], racc[:, 0:2],
                                          start=True, stop=True))
                with nc.Block() as blk:
                    @blk.vector
                    def _(v):
                        v.wait_ge(MM1.h, MM1.n)
                        vg(v, v.tensor_scalar(sthr[:, 0:1], p_tot[:, 0:1], 0.5 / (HID * D),
                                              0.5 * QEPS, ALU.mult, ALU.add))
                        vg(v, v.tensor_scalar(sthr[:, 1:2], p_tot[:, 1:2], 0.5 / (HID * D),
                                              0.5 * QEPS, ALU.mult, ALU.add))
                with nc.Block() as blk:
                    @blk.tensor
                    def _(te):
                        MM1.inc(te.matmul(p_thr[:, 0:1], ones_r128f[:], sthr[:, 0:1],
                                          start=True, stop=True))
                        MM1.inc(te.matmul(p_thr[:, 1:2], ones_r128f[:], sthr[:, 1:2],
                                          start=True, stop=True))
                with nc.Block() as blk:
                    @blk.vector
                    def _(v):
                        v.wait_ge(MM1.h, MM1.n)
                        vg(v, v.tensor_copy(thrq[:], p_thr[:]))
                        vg(v, v.tensor_scalar_mul(nthrq[:], p_thr[:], -1.0))

            with contextlib.ExitStack() as pps:
                pqr = [pps.enter_context(nc.psum_tensor("p_pq%d" % i, [128, 64], F32))
                       for i in range(4)]

                for wsrc, dst_s, ntile, cols, acc_i in ((wfc, fcT_s, 32, D, 0),
                                                        (wproj, pjT_s, 8, HID, 1)):
                    nblk = cols // 128
                    stride = HID if cols == D else D
                    with nc.Block() as blk:
                        dve_b, ldw_b, mm3_b, ev2_b = EV1.n, LDW.n, MM3.n, EV2.n

                        @blk.sync
                        def _(sync):
                            for cnt in range(ntile):
                                if cnt >= 2:
                                    sync.wait_ge(EV1.h, dve_b + cnt - 1)
                                LDW.inc(sync.dma_start(wtile[:, cnt % 2, 0:cols],
                                                       wsrc[cnt * 128:(cnt + 1) * 128, :]), 16)

                        @blk.vector
                        def _(v):
                            for cnt in range(ntile):
                                v.wait_ge(LDW.h, ldw_b + 16 * (cnt + 1))
                                if cnt >= 2:
                                    v.wait_ge(MM3.h, mm3_b + nblk * (cnt - 1))
                                w = wtile[:, cnt % 2, 0:cols]
                                v.tensor_single_scalar(gtile[:, 0:cols], w,
                                                       thrq[:, acc_i:acc_i + 1], ALU.is_gt)
                                vg(v, v.tensor_single_scalar(ltile[:, 0:cols], w,
                                                             nthrq[:, acc_i:acc_i + 1],
                                                             ALU.is_lt))
                                EV1.inc(v.tensor_sub(qtile[:, cnt % 2, 0:cols],
                                                     gtile[:, 0:cols], ltile[:, 0:cols]))
                                for b in range(nblk):
                                    tgl = cnt * nblk + b
                                    v.wait_ge(MM3.h, mm3_b + tgl + 1)
                                    EV2.inc(v.tensor_copy(
                                        wqT_p[:, b * stride + cnt * 128:
                                              b * stride + (cnt + 1) * 128],
                                        pqr[tgl % 4][:].bitcast(BF16)))
                            v.wait_ge(EV2.h, ev2_b + ntile * nblk)

                        @blk.tensor
                        def _(te):
                            for cnt in range(ntile):
                                te.wait_ge(EV1.h, dve_b + cnt + 1)
                                for b in range(nblk):
                                    tgl = cnt * nblk + b
                                    if tgl >= 4:
                                        te.wait_ge(EV2.h, ev2_b + tgl - 3)
                                    MM3.inc(te.transpose(
                                        pqr[tgl % 4][:].bitcast(BF16),
                                        qtile[:, cnt % 2, b * 128:(b + 1) * 128],
                                        eye_t[:]))

                    with nc.Block() as blk:
                        @blk.gpsimd
                        def _(g):
                            GP1.inc(g.dma_start(dst_s, wqT_p[:, 0:32768]), 16)
                            g.wait_ge(GP1.h, GP1.n)

        rows = {}
        for name, width in (("qb", D), ("kb", D), ("vb", D), ("ob", D),
                            ("td", D), ("tf", D), ("tmk", D), ("tmv", D),
                            ("tmr", D), ("lnw", HID), ("lnb", HID)):
            rows[name] = ps.enter_context(nc.sbuf_tensor("r_" + name, [1, width], F32))
        with nc.Block() as blk:
            @blk.sync
            def _(sync):
                for name in row_src:
                    LDX.inc(sync.dma_start(rows[name][:], row_src[name]), 16)
                sync.wait_ge(LDX.h, LDX.n)

        with contextlib.ExitStack() as pps:
            p_c = pps.enter_context(nc.psum_tensor("p_c", [128, 64], F32))
            with nc.Block() as blk:
                @blk.tensor
                def _(te):
                    jobs = [("qb", 0), ("kb", 8), ("td", 16), ("tf", 24),
                            ("tmk", 32), ("tmv", 40), ("tmr", 48), ("vb", 56)]
                    for name, base in jobs:
                        for j in range(ND):
                            MM1.inc(te.matmul(p_c[:, base + j:base + j + 1],
                                              rows[name][0:1, j * 128:(j + 1) * 128],
                                              ones_c1f[:], start=True, stop=True))
            with nc.Block() as blk:
                @blk.vector
                def _(v):
                    v.wait_ge(MM1.h, MM1.n)
                    vg(v, v.tensor_scalar_mul(qbd8[:], p_c[:, 0:8], 0.125))
                    v.tensor_copy(kbc[:], p_c[:, 8:16])
                    v.tensor_copy(tcol[:], p_c[:, 16:24])
                    v.tensor_copy(tcol2[:], p_c[:, 24:32])
                    v.tensor_copy(tmkc[:], p_c[:, 32:40])
                    v.tensor_copy(tmvc[:], p_c[:, 40:48])
                    v.tensor_copy(tmrc[:], p_c[:, 48:56])
                    v.tensor_copy(vbc[:], p_c[:, 56:64])
                    for dst, srcc in ((omkc, tmkc), (omvc, tmvc), (omrc, tmrc)):
                        vg(v, v.tensor_scalar(dst[:], srcc[:], -1.0, 1.0, ALU.mult, ALU.add))
            with nc.Block() as blk:
                @blk.scalar
                def _(scalar):
                    ag(scalar, scalar.activation(lamc[:], tcol[:], AF.Exp))
            with nc.Block() as blk:
                @blk.vector
                def _(v):
                    vg(v, v.tensor_scalar_mul(tcol[:], lamc[:], -1.0))
            with nc.Block() as blk:
                @blk.scalar
                def _(scalar):
                    ag(scalar, scalar.activation(lamc[:], tcol[:], AF.Exp))
                    ag(scalar, scalar.activation(euc[:], tcol2[:], AF.Exp))
            with nc.Block() as blk:
                @blk.tensor
                def _(te):
                    for j in range(32):
                        te.matmul(p_c[:, j:j + 1], rows["lnw"][0:1, j * 128:(j + 1) * 128],
                                  ones_c1f[:], start=True, stop=True)
                    for j in range(32):
                        MM1.inc(te.matmul(p_c[:, 32 + j:33 + j],
                                          rows["lnb"][0:1, j * 128:(j + 1) * 128],
                                          ones_c1f[:], start=True, stop=True))
            with nc.Block() as blk:
                @blk.vector
                def _(v):
                    v.wait_ge(MM1.h, MM1.n)
                    vg(v, v.tensor_copy(lnwc[:], p_c[:, 0:32]))
                    vg(v, v.tensor_copy(lnbc[:], p_c[:, 32:64]))
                    vg(v, v.tensor_copy(vbrow[:], rows["vb"][:]))
                    vg(v, v.tensor_copy(obrow[:], rows["ob"][:]))

        if "d_fcT" in dbg or "d_pjT" in dbg:
            with nc.Block() as blk:
                @blk.gpsimd
                def _(g):
                    if "d_fcT" in dbg:
                        STO.inc(g.dma_start(dbg["d_fcT"], fcT_s), 16)
                    if "d_pjT" in dbg:
                        STO.inc(g.dma_start(dbg["d_pjT"], pjT_s), 16)

    if upto == "P":
        _finish(nc, STO)
        st.close()
        return nc

    def emit_ln_pass(xin_ap, lnxT_h, lnx_h, xa_h, pT_hs, n_tiles=NT, tile0=0,
                     col_off=0, keep_x=False, dve_pre=None, dve_post=None, hb=None):
        with nc.Block() as blk:
            ldx_b, ex3_b, sta_b, rsd_b, nmr_b, lnx_b, trs_b, evt_b, sto_b = (
                LDX.n, EX3.n, STA.n, RSD.n, NMR.n, LNX.n, TRS.n, EVT.n, STO.n)

            @blk.sync
            def _(sync):
                sync.wait_ge(STO.h, sto_b)
                for i in range(n_tiles):
                    if not keep_x and i >= 2:
                        sync.wait_ge(LNX.h, lnx_b + i - 1)
                    r0 = (tile0 + i) * 128
                    LDX.inc(sync.dma_start(xa_h[:, i if keep_x else i % 2, :],
                                           xin_ap[r0:r0 + 128, :]), 16)

            @blk.vector
            def _(v):
                if dve_pre is not None:
                    dve_pre(v)
                for i in range(n_tiles):
                    par = i % 2
                    xv = xa_h[:, i if keep_x else par, :]
                    v.wait_ge(LDX.h, ldx_b + 16 * (i + 1))
                    vg(v, v.reduce_sum(s_sx[:, par:par + 1], xv, axis=AX.X))
                    v.wait_ge(EX3.h, ex3_b + i + 1)
                    STA.inc(ln_var_dve(v, par, INV_D))
                    v.wait_ge(RSD.h, rsd_b + i + 1)
                    NMR.inc(ln_nmr_dve(v, par))
                    for j in range(ND):
                        t = ND * i + j
                        v.wait_ge(TRS.h, trs_b + t + 1)
                        EVT.inc(v.tensor_copy(
                            lnxT_h[:, j, col_off + i * 128:col_off + (i + 1) * 128],
                            pT_hs[t % 2][:]))
                if dve_post is not None:
                    dve_post(v)

            @blk.scalar
            def _(scalar):
                for i in range(n_tiles):
                    par = i % 2
                    xv = xa_h[:, i if keep_x else par, :]
                    scalar.wait_ge(LDX.h, ldx_b + 16 * (i + 1))
                    if i >= 2:
                        scalar.wait_ge(STA.h, sta_b + i - 1)
                    ln_sq_act(scalar, xv, par)
                    EX3.inc(act_spacer(scalar))
                    scalar.wait_ge(STA.h, sta_b + i + 1)
                    RSD.inc(ln_r_act(scalar, par))
                    scalar.wait_ge(NMR.h, nmr_b + i + 1)
                    LNX.inc(scalar.activation(lnx_h[:, par, :], xv, AF.Identity,
                                              scale=s_r[:, par:par + 1],
                                              bias=s_nmr[:, par:par + 1]))

            @blk.tensor
            def _(te):
                for i in range(n_tiles):
                    par = i % 2
                    if hb is not None:
                        te.wait_ge(STA.h, sta_b + i + 1)
                        for _ in range(14):
                            te.matmul(hb[:, 0:128], eye_t[:], eye_t[:],
                                      start=True, stop=True)
                        te.wait_ge(RSD.h, rsd_b + i + 1)
                        for _ in range(8):
                            te.matmul(hb[:, 0:128], eye_t[:], eye_t[:],
                                      start=True, stop=True)
                    te.wait_ge(LNX.h, lnx_b + i + 1)
                    for j in range(ND):
                        t = ND * i + j
                        if t >= 2:
                            te.wait_ge(EVT.h, evt_b + t - 1)
                        TRS.inc(te.transpose(pT_hs[t % 2][:],
                                             lnx_h[:, par, j * 128:(j + 1) * 128],
                                             eye_t[:]))

    with contextlib.ExitStack() as As:
        lnxT = As.enter_context(nc.sbuf_tensor("lnxT", [128, ND, S], BF16))
        ctxT = As.enter_context(nc.sbuf_tensor("ctxT", [128, NPAIR, S], BF16))
        rden = As.enter_context(nc.sbuf_tensor("rden", [33, S], F32))

        with contextlib.ExitStack() as A1:
            xa1 = A1.enter_context(nc.sbuf_tensor("a_xa", [128, 2, D], F32))
            lnx1 = A1.enter_context(nc.sbuf_tensor("a_lnx", [128, 2, D], BF16))
            pT0 = A1.enter_context(nc.psum_tensor("a_pT0", [128, 128], BF16))
            pT1 = A1.enter_context(nc.psum_tensor("a_pT1", [128, 128], BF16))
            a1hb = A1.enter_context(nc.psum_tensor("a_hb", [128, 512], F32))

            with nc.Block() as blk:
                @blk.vector
                def _(v):
                    v.memset(rden[:], 0.0)

            emit_ln_pass(xq, lnxT, lnx1, xa1, (pT0, pT1), hb=a1hb)

        if "d_lnxT" in dbg:
            with nc.Block() as blk:
                @blk.gpsimd
                def _(g):
                    STO.inc(g.dma_start(dbg["d_lnxT"],
                                        lnxT[:, :, :].rearrange("p a b -> p (a b)")), 16)

        if upto == "A1":
            _finish(nc, STO)
            As.close()
            st.close()
            return nc

        with contextlib.ExitStack() as A2:
            a2 = lambda name, shape, dt: A2.enter_context(nc.sbuf_tensor(name, shape, dt))
            wqT = a2("wqT", [128, ND, 128], BF16)
            wkT = a2("wkT", [128, ND, 128], BF16)
            wvT = a2("wvT", [128, ND, 128], BF16)
            qTt = a2("qTt", [128, S], BF16)
            kTt = a2("kTt", [128, S], BF16)
            sq2 = a2("sq2", [128, S], BF16)
            vTt = a2("vTt", [128, S], BF16)
            vsb = a2("vsb", [128, NSK, 130], BF16)
            Ebuf = a2("Ebuf", [128, 2, 1024], BF16)
            nmax8 = a2("nmax8", [1, 16], F32)
            nB = a2("nB", [1, 4], F32)
            negBcol = a2("negBcol", [128, 1], F32)
            PS0 = A2.enter_context(nc.psum_tensor("PS0", [128, 1024], F32))
            PS1 = A2.enter_context(nc.psum_tensor("PS1", [128, 1024], F32))
            ct0a = A2.enter_context(nc.psum_tensor("ct0a", [128, 512], F32))
            ct1a = A2.enter_context(nc.psum_tensor("ct1a", [128, 512], F32))
            ct0b = A2.enter_context(nc.psum_tensor("ct0b", [128, 512], F32))
            ct1b = A2.enter_context(nc.psum_tensor("ct1b", [128, 512], F32))
            PS = (PS0, PS1)
            CT0 = (ct0a, ct0b)
            CT1 = (ct1a, ct1b)

            with nc.Block() as blk:
                @blk.gpsimd
                def _(g):
                    g.memset(vsb[:, :, 64:65], 1.0)
                    g.memset(vsb[:, :, 129:130], 1.0)

            for p in range(NPAIR):
                with nc.Block() as blk:
                    ldw_b, mm1_b, mm2_b, mm3_b = LDW.n, MM1.n, MM2.n, MM3.n
                    ex1_b, ev1_b, ev2_b, ev3_b = EX1.n, EV1.n, EV2.n, EV3.n
                    rsd_b = RSD.n

                    @blk.sync
                    def _(sync):
                        if p == 0:
                            sync.wait_ge(GP1.h, gp1_casts)
                        for wbuf, wsrc in ((wqT, wqbf), (wkT, wkbf), (wvT, wvbf)):
                            for j in range(ND):
                                LDW.inc(sync.dma_start_transpose(
                                    wbuf[:, j, :], wsrc[p * 128:(p + 1) * 128,
                                                        j * 128:(j + 1) * 128]), 16)

                    @blk.tensor
                    def _(te):
                        for idx in range(24):
                            wt = (wqT, wkT, wvT)[idx // 8]
                            nn = idx % 8
                            te.wait_ge(LDW.h, ldw_b + 128 * (idx // 8 + 1))
                            if idx >= 2:
                                te.wait_ge(EX1.h, ex1_b + idx - 1)
                            for j in range(ND):
                                mm = te.matmul(PS[idx % 2][:, 0:512], wt[:, j, :],
                                               lnxT[:, j, nn * 512:(nn + 1) * 512],
                                               start=(j == 0), stop=(j == ND - 1))
                            MM1.inc(mm)
                        for m in range(NSK):
                            te.wait_ge(EX1.h, ex1_b + 16 + (m // 4) + 1)
                            if m >= 2:
                                te.wait_ge(EV1.h, ev1_b + m - 1)
                            MM2.inc(te.transpose(
                                CT0[m % 2][:, 0:64].bitcast(BF16),
                                vTt[:, m * 128:(m + 1) * 128], eye_t[:]))
                        for c in range(16):
                            te.wait_ge(EV2.h, ev2_b + (1 if c < 8 else 2))
                            if c >= 2:
                                te.wait_ge(EV3.h, ev3_b + c - 1)
                            else:
                                te.wait_ge(EX1.h, ex1_b + 24)
                            MM3.inc(te.matmul(PS[c % 2][0:1, 0:512], ones_col_b[:],
                                              sq2[:, (c % 8) * 512:(c % 8 + 1) * 512],
                                              start=True, stop=True))
                        te.wait_ge(RSD.h, rsd_b + 1)
                        MM3.inc(te.matmul(PS[0][0:128, 512:513], ones_r128f[:],
                                          nB[0:1, 3:4], start=True, stop=True))

                    @blk.scalar
                    def _(scalar):
                        for idx in range(24):
                            scalar.wait_ge(MM1.h, mm1_b + idx + 1)
                            nn = idx % 8
                            if idx < 8:
                                EX1.inc(scalar.activation(
                                    qTt[:, nn * 512:(nn + 1) * 512], PS[idx % 2][:, 0:512],
                                    AF.Identity, scale=0.125, bias=qbd8[:, p:p + 1]))
                            elif idx < 16:
                                EX1.inc(scalar.activation(
                                    kTt[:, nn * 512:(nn + 1) * 512], PS[idx % 2][:, 0:512],
                                    AF.Identity, bias=kbc[:, p:p + 1]))
                            else:
                                EX1.inc(scalar.activation(
                                    vTt[:, nn * 512:(nn + 1) * 512], PS[idx % 2][:, 0:512],
                                    AF.Identity, bias=vbc[:, p:p + 1]))
                        scalar.wait_ge(EV3.h, ev3_b + 17)
                        ag(scalar, scalar.activation(nB[0:1, 2:3], nB[0:1, 1:2], AF.Ln))
                        RSD.inc(scalar.activation(nB[0:1, 3:4], nB[0:1, 2:3],
                                                  AF.Exp, scale=0.5))

                    @blk.vector
                    def _(v):
                        for m in range(NSK):
                            v.wait_ge(MM2.h, mm2_b + m + 1)
                            pv = CT0[m % 2][:, 0:64].bitcast(BF16)
                            v.tensor_copy(vsb[:, m, 0:64], pv[:, 0:64])
                            EV1.inc(v.tensor_copy(vsb[:, m, 65:129], pv[:, 64:128]))
                        v.wait_ge(EX1.h, ex1_b + 8)
                        EV2.inc(v.tensor_mul(sq2[:], qTt[:], qTt[:]))
                        for c in range(8):
                            v.wait_ge(MM3.h, mm3_b + c + 1)
                            EV3.inc(v.reduce_max(nmax8[0:1, c:c + 1], PS[c % 2][0:1, 0:512],
                                                 axis=AX.X))
                        v.wait_ge(EX1.h, ex1_b + 16)
                        EV2.inc(v.tensor_mul(sq2[:], kTt[:], kTt[:]))
                        for c in range(8, 16):
                            v.wait_ge(MM3.h, mm3_b + c + 1)
                            EV3.inc(v.reduce_max(nmax8[0:1, c:c + 1], PS[c % 2][0:1, 0:512],
                                                 axis=AX.X))
                        vg(v, v.tensor_copy(s_tmp[:, 0:1], s_tmp[:, 0:1]))
                        v.reduce_max(nB[0:1, 0:1], nmax8[0:1, 0:8], axis=AX.X)
                        vg(v, v.reduce_max(nB[0:1, 1:2], nmax8[0:1, 8:16], axis=AX.X))
                        EV3.inc(v.tensor_mul(nB[0:1, 1:2], nB[0:1, 1:2], nB[0:1, 0:1]))
                        v.wait_ge(MM3.h, mm3_b + 17)
                        v.tensor_scalar_mul(negBcol[:], PS[0][0:128, 512:513], -1.0)

                with nc.Block() as blk:
                    mm1_b, mm2_b, ex1_b, ev1_b = MM1.n, MM2.n, EX1.n, EV1.n

                    def emit_ctx(te, j):
                        n2, sk2 = divmod(j, 32)
                        par2 = n2 % 2
                        te.wait_ge(EX1.h, ex1_b + j + 1)
                        if sk2 == 0 and n2 >= 2:
                            te.wait_ge(EV1.h, ev1_b + n2 - 1)
                        st_ = (sk2 == 0)
                        sp = (sk2 == NSK - 1)
                        te.matmul(CT0[par2][0:65, :], vsb[:, sk2, 0:65],
                                  Ebuf[:, j % 2, 0:512], start=st_, stop=sp,
                                  tile_position=(0, 0))
                        MM2.inc(te.matmul(CT1[par2][0:65, :], vsb[:, sk2, 65:130],
                                          Ebuf[:, j % 2, 512:1024], start=st_, stop=sp,
                                          tile_position=(0, 0)))

                    @blk.tensor
                    def _(te):
                        for idx in range(256):
                            n, sk = divmod(idx, 32)
                            if idx >= 2:
                                te.wait_ge(EX1.h, ex1_b + idx - 1)
                            te.matmul(PS[idx % 2][:, 0:512],
                                      kTt[0:64, sk * 128:(sk + 1) * 128],
                                      qTt[0:64, n * 512:(n + 1) * 512],
                                      start=True, stop=True, tile_position=(0, 0))
                            MM1.inc(te.matmul(PS[idx % 2][:, 512:1024],
                                              kTt[64:128, sk * 128:(sk + 1) * 128],
                                              qTt[64:128, n * 512:(n + 1) * 512],
                                              start=True, stop=True, tile_position=(64, 0)))
                            if idx >= 1:
                                emit_ctx(te, idx - 1)
                        emit_ctx(te, 255)

                    @blk.scalar
                    def _(scalar):
                        for idx in range(256):
                            scalar.wait_ge(MM1.h, mm1_b + idx + 1)
                            if idx >= 2:
                                scalar.wait_ge(MM2.h, mm2_b + idx - 1)
                            EX1.inc(scalar.activation(Ebuf[:, idx % 2, :],
                                                      PS[idx % 2][:, 0:1024],
                                                      AF.Exp, bias=negBcol[:]))

                    @blk.vector
                    def _(v):
                        for n in range(NSQ):
                            par2 = n % 2
                            v.wait_ge(MM2.h, mm2_b + 32 * (n + 1))
                            v.reciprocal(rden[0:1, n * 512:(n + 1) * 512],
                                         CT0[par2][64:65, :])
                            v.reciprocal(rden[32:33, n * 512:(n + 1) * 512],
                                         CT1[par2][64:65, :])
                            v.tensor_copy(ctxT[0:64, p, n * 512:(n + 1) * 512],
                                          CT0[par2][0:64, :])
                            EV1.inc(v.tensor_copy(ctxT[64:128, p, n * 512:(n + 1) * 512],
                                                  CT1[par2][0:64, :]))

                with nc.Block() as blk:
                    mm3_b, ev2_b, ev1_t = MM3.n, EV2.n, EV1.n

                    @blk.tensor
                    def _(te):
                        te.wait_ge(EV1.h, ev1_t)
                        for gq in range(4):
                            if gq >= 2:
                                te.wait_ge(EV2.h, ev2_b + gq - 1)
                            for iq in range(8):
                                m = gq * 8 + iq
                                mm = te.matmul(PS[gq % 2][0:128, iq * 128:(iq + 1) * 128],
                                               sel_t[:],
                                               rden[0:33, m * 128:(m + 1) * 128],
                                               start=True, stop=True)
                            MM3.inc(mm)

                    @blk.vector
                    def _(v):
                        for gq in range(4):
                            v.wait_ge(MM3.h, mm3_b + gq + 1)
                            EV2.inc(v.tensor_tensor(ctxT[:, p, gq * 1024:(gq + 1) * 1024],
                                                    PS[gq % 2][0:128, 0:1024],
                                                    ctxT[:, p, gq * 1024:(gq + 1) * 1024],
                                                    ALU.mult))

            if "d_qT" in dbg:
                with nc.Block() as blk:
                    @blk.gpsimd
                    def _(g):
                        STO.inc(g.dma_start(dbg["d_qT"], qTt[:]), 16)
                        STO.inc(g.dma_start(dbg["d_kT"], kTt[:]), 16)

        if "d_ctxT" in dbg:
            with nc.Block() as blk:
                @blk.gpsimd
                def _(g):
                    STO.inc(g.dma_start(dbg["d_ctxT"],
                                        ctxT[:, :, :].rearrange("p a b -> p (a b)")), 16)
        if "d_rden" in dbg:
            with nc.Block() as blk:
                @blk.gpsimd
                def _(g):
                    STO.inc(g.dma_start(dbg["d_rden"], rden[:]), 16)

        with contextlib.ExitStack() as A5:
            woT = A5.enter_context(nc.sbuf_tensor("woT", [128, ND, D], BF16))
            xa5 = A5.enter_context(nc.sbuf_tensor("xa5", [128, 2, D], F32))
            xo5 = A5.enter_context(nc.sbuf_tensor("xo5", [128, 2, D], F32))
            po0 = A5.enter_context(nc.psum_tensor("po0", [128, 512], F32))
            po1 = A5.enter_context(nc.psum_tensor("po1", [128, 512], F32))
            PO = (po0, po1)

            with nc.Block() as blk:
                ldw_b, ldx_b, mm1_b, ev1_b, sto_b = LDW.n, LDX.n, MM1.n, EV1.n, STO.n

                @blk.sync
                def _(sync):
                    for j in range(ND):
                        LDW.inc(sync.dma_start_transpose(
                            woT[:, j, :], wobf[:, j * 128:(j + 1) * 128]), 16)
                    for m in range(NT):
                        if m >= 2:
                            sync.wait_ge(EV1.h, ev1_b + 2 * (m - 1))
                        LDX.inc(sync.dma_start(xa5[:, m % 2, :],
                                               xq[m * 128:(m + 1) * 128, :]), 16)

                @blk.tensor
                def _(te):
                    te.wait_ge(LDW.h, ldw_b + 128)
                    for m in range(NT):
                        for oc in range(2):
                            t = 2 * m + oc
                            if t >= 2:
                                te.wait_ge(EV1.h, ev1_b + t - 1)
                            for pp in range(NPAIR):
                                te.matmul(PO[oc][:, :], ctxT[:, pp, m * 128:(m + 1) * 128],
                                          woT[:, pp, oc * 512:(oc + 1) * 512],
                                          start=(pp == 0), stop=False)
                            MM1.inc(te.matmul(PO[oc][:, :], ones_r128b[:],
                                              obrow[0:1, oc * 512:(oc + 1) * 512],
                                              start=False, stop=True))

                @blk.vector
                def _(v):
                    for m in range(NT):
                        v.wait_ge(LDX.h, ldx_b + 16 * (m + 1))
                        if m >= 2:
                            v.wait_ge(STO.h, sto_b + 16 * (m - 1))
                        for oc in range(2):
                            t = 2 * m + oc
                            v.wait_ge(MM1.h, mm1_b + t + 1)
                            EV1.inc(v.tensor_add(xo5[:, m % 2, oc * 512:(oc + 1) * 512],
                                                 PO[oc][:, :],
                                                 xa5[:, m % 2, oc * 512:(oc + 1) * 512]))

                @blk.gpsimd
                def _(g):
                    for m in range(NT):
                        g.wait_ge(EV1.h, ev1_b + 2 * (m + 1))
                        STO.inc(g.dma_start(x1[m * 128:(m + 1) * 128, :],
                                            xo5[:, m % 2, :]), 16)
                    g.wait_ge(STO.h, STO.n)

    if "d_x1" in dbg:
        with nc.Block() as blk:
            @blk.gpsimd
            def _(g):
                g.wait_ge(STO.h, STO.n)
                STO.inc(g.dma_start(dbg["d_x1"], x1), 16)

    if upto == "A":
        _finish(nc, STO)
        st.close()
        return nc

    def emit_ffn(tag, xin_ap, xout_ap, dbg_w=None, dbg_bp=None):
        with contextlib.ExitStack() as Bs:
            bsb = lambda name, shape, dt: Bs.enter_context(
                nc.sbuf_tensor(tag + name, shape, dt))
            WfcT = bsb("WfcT", [128, ND, HID], BF16)
            WpjT = bsb("WpjT", [128, NHT, D], BF16)
            bpc = bsb("bpc", [128, ND], F32)
            xa = bsb("xa", [128, 3, D], F32)
            xo = bsb("xo", [128, 2, D], F32)
            lnx = bsb("lnx", [128, 2, D], BF16)
            lxT = bsb("lxT", [128, 2, ND, 128], BF16)
            hbuf = bsb("h", [128, HID], BF16)
            hnb = bsb("hn", [128, HID], BF16)
            hnT = bsb("hnT", [128, 2, NHT, 128], BF16)
            sumh8 = bsb("sumh8", [128, 2, NHC], F32)
            sumq8 = bsb("sumq8", [128, 2, NHC], F32)
            pT0 = Bs.enter_context(nc.psum_tensor(tag + "pT0", [128, 128], BF16))
            pT1 = Bs.enter_context(nc.psum_tensor(tag + "pT1", [128, 128], BF16))
            pF0 = Bs.enter_context(nc.psum_tensor(tag + "pF0", [128, 512], F32))
            pF1 = Bs.enter_context(nc.psum_tensor(tag + "pF1", [128, 512], F32))
            pF2 = Bs.enter_context(nc.psum_tensor(tag + "pF2", [128, 512], F32))
            pF3 = Bs.enter_context(nc.psum_tensor(tag + "pF3", [128, 512], F32))
            pP0 = Bs.enter_context(nc.psum_tensor(tag + "pP0", [128, 512], F32))
            pP1 = Bs.enter_context(nc.psum_tensor(tag + "pP1", [128, 512], F32))
            pB = pP0[:, 0:ND]
            pTs = (pT0, pT1)
            pFs = (pF0, pF1, pF2, pF3)
            pPs = (pP0, pP1)

            with nc.Block() as blk:
                ldw_b = LDW.n

                @blk.sync
                def _(sync):
                    sync.wait_ge(STO.h, STO.n)
                    wfc_flat = WfcT[:, :, :].rearrange("p a b -> p (a b)")
                    wpj_flat = WpjT[:, :, :].rearrange("p a b -> p (a b)")
                    for c8 in range(8):
                        LDW.inc(sync.dma_start(wfc_flat[:, c8 * 4096:(c8 + 1) * 4096],
                                               fcT_s[:, c8 * 4096:(c8 + 1) * 4096]), 16)
                    for c8 in range(8):
                        LDW.inc(sync.dma_start(wpj_flat[:, c8 * 4096:(c8 + 1) * 4096],
                                               pjT_s[:, c8 * 4096:(c8 + 1) * 4096]), 16)
                    sync.wait_ge(LDW.h, LDW.n)

                @blk.tensor
                def _(te):
                    for c16 in range(16):
                        te.wait_ge(LDW.h, ldw_b + 16 * c16)
                        for _ in range(12):
                            te.matmul(pF3[:, 0:128], eye_t[:], eye_t[:],
                                      start=True, stop=True)
            with nc.Block() as blk:
                @blk.tensor
                def _(te):
                    for dt_ in range(ND):
                        for t in range(NHT):
                            mm = te.matmul(pB[:, dt_:dt_ + 1],
                                           WpjT[:, t, dt_ * 128:(dt_ + 1) * 128],
                                           lnbc[:, t:t + 1],
                                           start=(t == 0), stop=(t == NHT - 1))
                            if t == NHT - 1:
                                MM1.inc(mm)
            with nc.Block() as blk:
                @blk.vector
                def _(v):
                    v.wait_ge(MM1.h, MM1.n)
                    vg(v, v.tensor_copy(bpc[:], pB))
            with nc.Block() as blk:
                @blk.gpsimd
                def _(g):
                    GP1.inc(g.dma_start(
                        bp_s.rearrange("o (j p) -> p (o j)", p=128), bpc[:],
                        allow_slow_non_contiguous=True), 16)
                    g.wait_ge(GP1.h, GP1.n)
                    GP1.inc(g.dma_start(bprow[:], bp_s), 16)
                    g.wait_ge(GP1.h, GP1.n)

            if dbg_w is not None:
                with nc.Block() as blk:
                    @blk.gpsimd
                    def _(g):
                        STO.inc(g.dma_start(dbg_w, WfcT[:, :, :].rearrange("p a b -> p (a b)")), 16)
                        STO.inc(g.dma_start(dbg_bp, bprow[:]), 16)
                        g.wait_ge(STO.h, STO.n)
            with nc.Block() as blk:
                (ldx_b, ex3_b, sta_b, rsd_b, nmr_b, lnx_b, trs_b, evt_b, mm1_b,
                 mm2_b, ex1_b, ex2_b, ev1_b, sto_b) = (
                    LDX.n, EX3.n, STA.n, RSD.n, NMR.n, LNX.n, TRS.n, EVT.n,
                    MM1.n, MM2.n, EX1.n, EX2.n, EV1.n, STO.n)
                stax = lambda k: 2 * k + 1
                stah = lambda i: 2 * i + 2
                lx_idx, hn_idx = {}, {}
                _seq = 0
                for _k in range(NT + 1):
                    if _k < NT:
                        for _j in range(ND):
                            lx_idx[(_k, _j)] = _seq
                            _seq += 1
                    if _k >= 1:
                        for _tt in range(NHT):
                            hn_idx[(_k - 1, _tt)] = _seq
                            _seq += 1

                @blk.sync
                def _(sync):
                    for i in range(NT):
                        if i >= 3:
                            sync.wait_ge(EV1.h, ev1_b + 2 * (i - 2))
                        LDX.inc(sync.dma_start(xa[:, i % 3, :],
                                               xin_ap[i * 128:(i + 1) * 128, :]), 16)

                @blk.gpsimd
                def _(g):
                    for i in range(NT):
                        g.wait_ge(EV1.h, ev1_b + 2 * (i + 1))
                        STO.inc(g.dma_start(xout_ap[i * 128:(i + 1) * 128, :],
                                            xo[:, i % 2, :]), 16)
                    g.wait_ge(STO.h, STO.n)

                @blk.vector
                def _(v):
                    for k in range(NT + 1):
                        par = k % 2
                        if k < NT:
                            v.wait_ge(LDX.h, ldx_b + 16 * (k + 1))
                            vg(v, v.reduce_sum(s_sx[:, par:par + 1], xa[:, k % 3, :],
                                               axis=AX.X))
                            v.wait_ge(EX3.h, ex3_b + k + 1)
                            STA.inc(ln_var_dve(v, par, INV_D))
                            v.wait_ge(RSD.h, rsd_b + stax(k))
                            NMR.inc(ln_nmr_dve(v, par))
                            for j in range(ND):
                                t = lx_idx[(k, j)]
                                v.wait_ge(TRS.h, trs_b + t + 1)
                                EVT.inc(v.tensor_copy(lxT[:, par, j, :], pTs[t % 2][:]))
                        if k >= 1:
                            i = k - 1
                            pi = i % 2
                            for tt in range(NHT):
                                t = hn_idx[(i, tt)]
                                v.wait_ge(TRS.h, trs_b + t + 1)
                                EVT.inc(v.tensor_scalar_mul(hnT[:, pi, tt, :],
                                                            pTs[t % 2][:],
                                                            lnwc[:, tt:tt + 1]))
                        if k < NT:
                            for hc in range(NHC):
                                v.wait_ge(EX1.h, ex1_b + 16 * k + hc + 1)
                                hs = v.scalar_tensor_tensor(
                                    dumpb[:, 1024:1536], hbuf[:, hc * 512:(hc + 1) * 512], 1.0,
                                    hbuf[:, hc * 512:(hc + 1) * 512], ALU.mult, ALU.mult,
                                    accum_out=sumq8[:, par, hc:hc + 1])
                            vg(v, hs)
                            v.wait_ge(EX2.h, ex2_b + k + 1)
                            v.reduce_sum(s_sx[:, par:par + 1], sumh8[:, par, :], axis=AX.X)
                            vg(v, v.reduce_sum(s_sq[:, par:par + 1], sumq8[:, par, :],
                                               axis=AX.X))
                            STA.inc(ln_var_dve(v, par, INV_H))
                            v.wait_ge(RSD.h, rsd_b + stah(k))
                            NMR.inc(ln_nmr_dve(v, par))
                        if k >= 1:
                            i = k - 1
                            pi = i % 2
                            if i >= 2:
                                v.wait_ge(STO.h, sto_b + 16 * (i - 1))
                            for oc in range(2):
                                v.wait_ge(MM2.h, mm2_b + 2 * i + oc + 1)
                                EV1.inc(v.tensor_add(xo[:, pi, oc * 512:(oc + 1) * 512],
                                                     pPs[oc][:, :],
                                                     xa[:, i % 3, oc * 512:(oc + 1) * 512]))

                @blk.scalar
                def _(scalar):
                    for k in range(NT + 1):
                        par = k % 2
                        if k < NT:
                            scalar.wait_ge(LDX.h, ldx_b + 16 * (k + 1))
                            if k >= 2:
                                scalar.wait_ge(STA.h, sta_b + stah(k - 2))
                            ln_sq_act(scalar, xa[:, k % 3, :], par)
                            EX3.inc(act_spacer(scalar))
                            scalar.wait_ge(STA.h, sta_b + stax(k))
                            RSD.inc(ln_r_act(scalar, par))
                            scalar.wait_ge(NMR.h, nmr_b + stax(k))
                            if k >= 2:
                                scalar.wait_ge(TRS.h, trs_b + lx_idx[(k - 2, ND - 1)] + 1)
                            LNX.inc(scalar.activation(lnx[:, par, :], xa[:, k % 3, :],
                                                      AF.Identity,
                                                      scale=s_r[:, par:par + 1],
                                                      bias=s_nmr[:, par:par + 1]))
                        if k >= 1:
                            i = k - 1
                            pi = i % 2
                            scalar.wait_ge(NMR.h, nmr_b + stah(i))
                            if i >= 1:
                                scalar.wait_ge(TRS.h, trs_b + hn_idx[(i - 1, NHT - 1)] + 1)
                            for hc in range(NHC):
                                EX1.inc(scalar.activation(hnb[:, hc * 512:(hc + 1) * 512],
                                                          hbuf[:, hc * 512:(hc + 1) * 512],
                                                          AF.Identity,
                                                          scale=s_r[:, pi:pi + 1],
                                                          bias=s_nmr[:, pi:pi + 1]))
                        if k < NT:
                            for hc in range(NHC):
                                scalar.wait_ge(MM1.h, mm1_b + 8 * k + hc + 1)
                                EX1.inc(scalar.activation(
                                    hbuf[:, hc * 512:(hc + 1) * 512],
                                    pFs[hc % 4][:, :], AF.Gelu,
                                    accum_out=sumh8[:, par, hc:hc + 1]))
                            EX2.inc(act_spacer(scalar))
                            scalar.wait_ge(STA.h, sta_b + stah(k))
                            RSD.inc(ln_r_act(scalar, par))

                @blk.tensor
                def _(te):
                    for k in range(NT + 1):
                        par = k % 2
                        if k < NT:
                            te.wait_ge(LNX.h, lnx_b + k + 1)
                            for j in range(ND):
                                t = lx_idx[(k, j)]
                                if t >= 2:
                                    te.wait_ge(EVT.h, evt_b + t - 1)
                                TRS.inc(te.transpose(pTs[t % 2][:],
                                                     lnx[:, par, j * 128:(j + 1) * 128],
                                                     eye_t[:]))
                            te.wait_ge(EVT.h, evt_b + lx_idx[(k, ND - 1)] + 1)
                            for hc in range(NHC):
                                g_ = 8 * k + hc
                                if g_ >= 4:
                                    ip, hp = divmod(g_ - 4, 8)
                                    te.wait_ge(EX1.h, ex1_b + 16 * ip + hp + 1)
                                for j in range(ND):
                                    mm = te.matmul(pFs[hc % 4][:, :],
                                                   lxT[:, par, j, :],
                                                   WfcT[:, j, hc * 512:(hc + 1) * 512],
                                                   start=(j == 0), stop=(j == ND - 1))
                                MM1.inc(mm)
                        if k >= 1:
                            i = k - 1
                            pi = i % 2
                            for tt in range(NHT):
                                t = hn_idx[(i, tt)]
                                te.wait_ge(EX1.h, ex1_b + 16 * i + 8 + (tt // 4) + 1)
                                if t >= 2:
                                    te.wait_ge(EVT.h, evt_b + t - 1)
                                TRS.inc(te.transpose(pTs[t % 2][:],
                                                     hnb[:, tt * 128:(tt + 1) * 128],
                                                     eye_t[:]))
                            te.wait_ge(EVT.h, evt_b + hn_idx[(i, NHT - 1)] + 1)
                            for oc in range(2):
                                t2 = 2 * i + oc
                                if t2 >= 2:
                                    te.wait_ge(EV1.h, ev1_b + t2 - 1)
                                for tt in range(NHT):
                                    te.matmul(pPs[oc][:, :], hnT[:, pi, tt, :],
                                              WpjT[:, tt, oc * 512:(oc + 1) * 512],
                                              start=(tt == 0), stop=False)
                                MM2.inc(te.matmul(pPs[oc][:, :], ones_r128b[:],
                                                  bprow[0:1, oc * 512:(oc + 1) * 512],
                                                  start=False, stop=True))

    emit_ffn("b_", x1, x2, dbg.get("d_wfcT"), dbg.get("d_bp"))
    if "d_x2" in dbg:
        with nc.Block() as blk:
            @blk.gpsimd
            def _(g):
                g.wait_ge(STO.h, STO.n)
                STO.inc(g.dma_start(dbg["d_x2"], x2), 16)
    if upto == "B":
        _finish(nc, STO)
        st.close()
        return nc

    with contextlib.ExitStack() as Cs:
        csb = lambda name, shape, dt: Cs.enter_context(nc.sbuf_tensor(name, shape, dt))
        WkT4 = csb("WkT4", [128, ND, D], BF16)
        WvT4 = csb("WvT4", [128, ND, D], BF16)
        WrT4 = csb("WrT4", [128, ND, D], BF16)
        WoT4 = csb("WoT4", [128, ND, D], BF16)
        lamt = csb("lamt", [128, ND, 512], F32)
        ones512 = csb("ones512", [128, 512], F32)
        lnxTc = csb("lnxTc", [128, ND, 513], BF16)
        xkT = csb("xkT", [128, ND, 512], BF16)
        xvT = csb("xvT", [128, ND, 512], BF16)
        xrT = csb("xrT", [128, ND, 512], BF16)
        Abuf = csb("Abuf", [128, ND, 513], F32)
        Bbuf = csb("Bbuf", [128, ND, 513], F32)
        Ec = csb("Ec", [128, 512], F32)
        Vc = csb("Vc", [128, 512], F32)
        Xc = csb("Xc", [128, 512], F32)
        SRc = csb("SRc", [128, 512], F32)
        TN = csb("TN", [128, 512], F32)
        TD = csb("TD", [128, 512], F32)
        rwT = csb("rwT", [128, ND, 512], BF16)
        xac = csb("c_xa", [128, 4, D], F32)
        xoc = csb("c_xo", [128, 2, D], F32)
        lnxc = csb("c_lnx", [128, 2, D], BF16)
        cpT0 = Cs.enter_context(nc.psum_tensor("c_pT0", [128, 128], BF16))
        cpT1 = Cs.enter_context(nc.psum_tensor("c_pT1", [128, 128], BF16))
        cpK = Cs.enter_context(nc.psum_tensor("c_pK", [128, 512], F32))
        cpV = Cs.enter_context(nc.psum_tensor("c_pV", [128, 512], F32))
        cpR = Cs.enter_context(nc.psum_tensor("c_pR", [128, 512], F32))
        cpO0 = Cs.enter_context(nc.psum_tensor("c_pO0", [128, 512], F32))
        cpO1 = Cs.enter_context(nc.psum_tensor("c_pO1", [128, 512], F32))
        chb = Cs.enter_context(nc.psum_tensor("c_hb", [128, 512], F32))
        cpTs = (cpT0, cpT1)
        cpOs = (cpO0, cpO1)

        with nc.Block() as blk:
            sto_b0 = STO.n

            @blk.sync
            def _(sync):
                sync.wait_ge(STO.h, sto_b0)
                for wsrc, wdst in ((bwkbf, WkT4), (bwvbf, WvT4), (bwrbf, WrT4), (bwobf, WoT4)):
                    for j in range(ND):
                        LDW.inc(sync.dma_start_transpose(
                            wdst[:, j, :], wsrc[:, j * 128:(j + 1) * 128]), 16)
                sync.wait_ge(LDW.h, LDW.n)

            @blk.gpsimd
            def _(g):
                g.wait_ge(STO.h, sto_b0)
                g.memset(ones512[:], 1.0)
                g.memset(lnxTc[:, :, 0:1], 0.0)
                g.memset(Abuf[:, :, 512:513], 0.0)
                g.memset(Bbuf[:, :, 512:513], 0.0)

        with nc.Block() as blk:
            @blk.vector
            def _(v):
                for j in range(ND):
                    v.tensor_scalar_mul(lamt[:, j, :], ones512[:], lamc[:, j:j + 1])

        for c in range(NSQ):
            def c_pre(v, c=c):
                if c > 0:
                    v.tensor_copy(lnxTc[:, :, 0:1], lnxTc[:, :, 512:513])
                v.tensor_copy(Abuf[:, :, 0:1], Abuf[:, :, 512:513])
                v.tensor_copy(Bbuf[:, :, 0:1], Bbuf[:, :, 512:513])

            def c_post(v, c=c):
                for j in range(ND):
                    cur = lnxTc[:, j, 1:513]
                    sh = lnxTc[:, j, 0:512]
                    for dst, tc_, oc_ in ((xkT, tmkc, omkc), (xvT, tmvc, omvc),
                                          (xrT, tmrc, omrc)):
                        v.tensor_scalar_mul(TN[:], sh, oc_[:, j:j + 1])
                        v.scalar_tensor_tensor(dst[:, j, :], cur, tc_[:, j:j + 1],
                                               TN[:], ALU.mult, ALU.add)

            emit_ln_pass(x2, lnxTc, lnxc, xac, cpTs, n_tiles=4, tile0=4 * c,
                         col_off=1, keep_x=True, dve_pre=c_pre, dve_post=c_post,
                         hb=chb)

            with nc.Block() as blk:
                mm1_b, ex1_b, ex2_b, ev1_b, ev2_b, ev3_b = (
                    MM1.n, EX1.n, EX2.n, EV1.n, EV2.n, EV3.n)

                @blk.tensor
                def _(te):
                    for j in range(ND):
                        if j >= 1:
                            te.wait_ge(EX1.h, ex1_b + j)
                            te.wait_ge(EV2.h, ev2_b + j)
                            te.wait_ge(EX2.h, ex2_b + j)
                        for jin in range(ND):
                            mmk = te.matmul(cpK[:, :], WkT4[:, jin, j * 128:(j + 1) * 128],
                                            xkT[:, jin, :], start=(jin == 0), stop=(jin == ND - 1))
                        MM1.inc(mmk)
                        for jin in range(ND):
                            mmv = te.matmul(cpV[:, :], WvT4[:, jin, j * 128:(j + 1) * 128],
                                            xvT[:, jin, :], start=(jin == 0), stop=(jin == ND - 1))
                        MM1.inc(mmv)
                        for jin in range(ND):
                            mmr = te.matmul(cpR[:, :], WrT4[:, jin, j * 128:(j + 1) * 128],
                                            xrT[:, jin, :], start=(jin == 0), stop=(jin == ND - 1))
                        MM1.inc(mmr)

                @blk.scalar
                def _(scalar):
                    for j in range(ND):
                        scalar.wait_ge(MM1.h, mm1_b + 3 * j + 1)
                        if j >= 1:
                            scalar.wait_ge(EV3.h, ev3_b + j)
                        EX1.inc(scalar.activation(Ec[:], cpK[:, :], AF.Exp))
                        scalar.wait_ge(MM1.h, mm1_b + 3 * j + 3)
                        if j >= 1:
                            scalar.wait_ge(EV1.h, ev1_b + j)
                        EX2.inc(scalar.activation(SRc[:], cpR[:, :], AF.Tanh, scale=0.5))

                @blk.vector
                def _(v):
                    for j in range(ND):
                        v.wait_ge(MM1.h, mm1_b + 3 * j + 2)
                        EV2.inc(v.tensor_copy(Vc[:], cpV[:, :]))
                        v.wait_ge(EX1.h, ex1_b + j + 1)
                        v.tensor_mul(Xc[:], Ec[:], Vc[:])
                        v.tensor_tensor_scan(Abuf[:, j, 1:513], lamt[:, j, :], Xc[:],
                                             Abuf[:, j, 0:1], ALU.mult, ALU.add)
                        v.tensor_tensor_scan(Bbuf[:, j, 1:513], lamt[:, j, :], Ec[:],
                                             Bbuf[:, j, 0:1], ALU.mult, ALU.add)
                        v.scalar_tensor_tensor(TN[:], Xc[:], euc[:, j:j + 1],
                                               Abuf[:, j, 0:512], ALU.mult, ALU.add)
                        EV3.inc(v.scalar_tensor_tensor(TD[:], Ec[:], euc[:, j:j + 1],
                                                       Bbuf[:, j, 0:512], ALU.mult, ALU.add))
                        v.reciprocal(TD[:], TD[:])
                        v.tensor_mul(TN[:], TN[:], TD[:])
                        v.wait_ge(EX2.h, ex2_b + j + 1)
                        v.tensor_scalar(SRc[:], SRc[:], 0.5, 0.5, ALU.mult, ALU.add)
                        EV1.inc(v.tensor_mul(rwT[:, j, :], TN[:], SRc[:]))

            with nc.Block() as blk:
                mm2_b, ev1_b, sto_b = MM2.n, EV1.n, STO.n

                @blk.tensor
                def _(te):
                    for m in range(4):
                        for oc in range(2):
                            t = 2 * m + oc
                            if t >= 2:
                                te.wait_ge(EV1.h, ev1_b + t - 1)
                            for j in range(ND):
                                mm = te.matmul(cpOs[oc][:, :], rwT[:, j, m * 128:(m + 1) * 128],
                                               WoT4[:, j, oc * 512:(oc + 1) * 512],
                                               start=(j == 0), stop=(j == ND - 1))
                            MM2.inc(mm)

                @blk.vector
                def _(v):
                    for m in range(4):
                        v.wait_ge(STO.h, sto_b + 16 * (m - 1) if m >= 2 else sto_b)
                        for oc in range(2):
                            t = 2 * m + oc
                            v.wait_ge(MM2.h, mm2_b + t + 1)
                            EV1.inc(v.tensor_add(xoc[:, m % 2, oc * 512:(oc + 1) * 512],
                                                 cpOs[oc][:, :],
                                                 xac[:, m, oc * 512:(oc + 1) * 512]))

                @blk.gpsimd
                def _(g):
                    for m in range(4):
                        g.wait_ge(EV1.h, ev1_b + 2 * (m + 1))
                        STO.inc(g.dma_start(
                            x3[(4 * c + m) * 128:(4 * c + m + 1) * 128, :],
                            xoc[:, m % 2, :]), 16)
                    if c == NSQ - 1:
                        g.wait_ge(STO.h, STO.n)

    if "d_x3" in dbg:
        with nc.Block() as blk:
            @blk.gpsimd
            def _(g):
                g.wait_ge(STO.h, STO.n)
                STO.inc(g.dma_start(dbg["d_x3"], x3), 16)
    if upto == "C":
        _finish(nc, STO)
        st.close()
        return nc

    emit_ffn("d_", x3, out)

    _finish(nc, STO)
    st.close()
    return nc


def _finish(nc, STO):
    with nc.Block() as blk:
        @blk.sync
        def _(sync):
            sync.wait_ge(STO.h, STO.n)


_SEL = None
_EYE = None


def _consts():
    global _SEL, _EYE
    if _SEL is None:
        s = np.zeros((33, 128), dtype=np.float32)
        s[0, 0:64] = 1.0
        s[32, 64:128] = 1.0
        _SEL = s
        _EYE = np.eye(128, dtype=ml_dtypes.bfloat16)
    return _SEL, _EYE


def make_in_maps(inputs):
    f32 = np.float32
    q = np.ascontiguousarray(np.asarray(inputs["L0_attn_a_query"], dtype=f32))
    selv, eyev = _consts()
    shared = {
        "wq": np.ascontiguousarray(np.asarray(inputs["L0_attn_a_q_w"], f32)),
        "wk": np.ascontiguousarray(np.asarray(inputs["L0_attn_a_k_w"], f32)),
        "wv": np.ascontiguousarray(np.asarray(inputs["L0_attn_a_v_w"], f32)),
        "wo": np.ascontiguousarray(np.asarray(inputs["L0_attn_a_out_w"], f32)),
        "qb": np.asarray(inputs["L0_attn_a_q_b"], f32).reshape(1, -1),
        "kb": np.asarray(inputs["L0_attn_a_k_b"], f32).reshape(1, -1),
        "vb": np.asarray(inputs["L0_attn_a_v_b"], f32).reshape(1, -1),
        "ob": np.asarray(inputs["L0_attn_a_out_b"], f32).reshape(1, -1),
        "bwk": np.ascontiguousarray(np.asarray(inputs["L0_attn_b_Wk"], f32)),
        "bwv": np.ascontiguousarray(np.asarray(inputs["L0_attn_b_Wv"], f32)),
        "bwr": np.ascontiguousarray(np.asarray(inputs["L0_attn_b_Wr"], f32)),
        "bwo": np.ascontiguousarray(np.asarray(inputs["L0_attn_b_Wo"], f32)),
        "td": np.asarray(inputs["L0_attn_b_time_decay"], f32).reshape(1, -1),
        "tf": np.asarray(inputs["L0_attn_b_time_first"], f32).reshape(1, -1),
        "tmk": np.asarray(inputs["L0_attn_b_time_mix_k"], f32).reshape(1, -1),
        "tmv": np.asarray(inputs["L0_attn_b_time_mix_v"], f32).reshape(1, -1),
        "tmr": np.asarray(inputs["L0_attn_b_time_mix_r"], f32).reshape(1, -1),
        "wfc": np.ascontiguousarray(np.asarray(inputs["L0_ffn_W_fc"], f32)),
        "lnw": np.asarray(inputs["L0_ffn_ln_w"], f32).reshape(1, -1),
        "lnb": np.asarray(inputs["L0_ffn_ln_b"], f32).reshape(1, -1),
        "wproj": np.ascontiguousarray(np.asarray(inputs["L0_ffn_W_proj"], f32)),
        "eye": eyev,
        "sel": selv,
    }
    maps = []
    for b in range(NB):
        m = dict(shared)
        m["xq"] = np.ascontiguousarray(q[b])
        maps.append(m)
    return maps


def kernel(**inputs):
    nc = build_program(upto="D")
    in_maps = make_in_maps(inputs)
    res = run_bass_kernel_spmd(nc, in_maps, core_ids=list(range(NB)))
    return np.stack([np.asarray(r["out"], dtype=np.float32) for r in res.results], axis=0)
